# version 08 (final)
# baseline (speedup 1.0000x reference)
import math
import numpy as np
import ml_dtypes
from contextlib import ExitStack
import concourse.bass as bass
import concourse.mybir as mybir
from concourse.bass_utils import run_bass_kernel_spmd

F32, BF16 = mybir.dt.float32, mybir.dt.bfloat16
AF, ALU, AX = mybir.ActivationFunctionType, mybir.AluOpType, mybir.AxisListType

D, S, NH, DH, NT = 2048, 4096, 16, 128, 16
NE, FF = 8, 1024
EPS = 1e-5
ISQ = 1.0 / math.sqrt(128.0)
NEG = -1e9
RD = 1.0 / 2048.0
bf = ml_dtypes.bfloat16

ENGS = ("sync", "gpsimd", "tensor", "scalar", "vector")


class Ctr:
    def __init__(self, ctx, nc, name):
        self.h = ctx.enter_context(nc.semaphore(name))
        self.n = 0

    def bump(self, k=1):
        self.n += k
        return self.n


class Prog:

    def __init__(self, ctx, nc):
        self.nc = nc
        self.ops = {e: [] for e in ENGS}
        self.c = {e: Ctr(ctx, nc, "c_" + e) for e in ENGS}
        self.so = Ctr(ctx, nc, "so")

    def emit(self, eng, fn):
        self.ops[eng].append(fn)

    def wait(self, eng, ctr, val=None):
        t = ctr.n if val is None else val
        if t > 0:
            self.emit(eng, lambda e, _c=ctr.h, _t=t: e.wait_ge(_c, _t))

    def op(self, eng, fn, inc=True, k=1, ctr=None):
        c = ctr if ctr is not None else self.c[eng]
        if inc:
            self.emit(eng, lambda e, _f=fn, _c=c.h, _k=k: _f(e).then_inc(_c, _k))
            return c.bump(k)
        self.emit(eng, lambda e, _f=fn: _f(e))
        return c.n

    def dma(self, eng, out, in_, ctr=None, **kw):
        return self.op(eng, lambda e: e.dma_start(out, in_, **kw), k=16, ctr=ctr)

    def selfwait(self, eng):
        self.wait(eng, self.c[eng])

    def join(self, eng, others=None):
        for o in (others or ENGS):
            if o != eng:
                self.wait(eng, self.c[o])

    def barrier(self):
        for e in ENGS:
            self.join(e)

    def finalize(self, block):
        for eng in ENGS:
            todo = self.ops[eng]
            getattr(block, eng)(lambda e, _t=todo: [f(e) for f in _t])


def build_program(debug=0, phases=99):
    nc = bass.Bass("TRN2", target_bir_lowering=False)
    xT = nc.dram_tensor("xT", [8, 128, NT * 512], F32, kind="ExternalInput").ap()
    wq = nc.dram_tensor("wq", [NH, 128, NT * 128], F32, kind="ExternalInput").ap()
    wk = nc.dram_tensor("wk", [NH, 128, NT * 128], F32, kind="ExternalInput").ap()
    wv = nc.dram_tensor("wv", [NH, 128, NT * 128], F32, kind="ExternalInput").ap()
    bqkv = nc.dram_tensor("bqkv", [3, 128, NH], F32, kind="ExternalInput").ap()
    lngb = nc.dram_tensor("lngb", [2, 128, NT], F32, kind="ExternalInput").ap()
    wo = nc.dram_tensor("wo", [NT, 128, NH * 128], F32, kind="ExternalInput").ap()
    boT = nc.dram_tensor("boT", [128, NT], F32, kind="ExternalInput").ap()
    rw = nc.dram_tensor("rw", [128, NT, NE], F32, kind="ExternalInput").ap()
    wg = nc.dram_tensor("wg", [NE, 8, 128, NT * 128], F32, kind="ExternalInput").ap()
    wu = nc.dram_tensor("wu", [NE, 8, 128, NT * 128], F32, kind="ExternalInput").ap()
    wd = nc.dram_tensor("wd", [NE, NT, 128, 8 * 128], F32, kind="ExternalInput").ap()
    eye = nc.dram_tensor("eye", [128, 128], BF16, kind="ExternalInput").ap()
    eyef = nc.dram_tensor("eyef", [128, 128], F32, kind="ExternalInput").ap()
    rdg = nc.dram_tensor("rdg", [128, 4 * 512], BF16, kind="ExternalInput").ap()
    mska = nc.dram_tensor("mska", [128, 16], F32, kind="ExternalInput").ap()
    mskb = nc.dram_tensor("mskb", [128, 32], F32, kind="ExternalInput").ap()
    out = nc.dram_tensor("out", [2, NT, 128, 512], F32, kind="ExternalOutput").ap()
    kvK = nc.dram_tensor("kvK", [NH, 128, S], BF16,
                         kind="ExternalOutput" if debug >= 1 else "Internal").ap()
    kvV = nc.dram_tensor("kvV", [NH, 128, S], BF16,
                         kind="ExternalOutput" if debug >= 1 else "Internal").ap()
    kvQ = nc.dram_tensor("kvQ", [NH, 128, 1024], BF16,
                         kind="ExternalOutput" if debug >= 1 else "Internal").ap()
    davn = nc.dram_tensor("davn", [NH, 128, 1024], BF16,
                          kind="ExternalOutput") if debug >= 2 else None
    dx2 = nc.dram_tensor("dx2", [NT, 128, 1024], F32, kind="ExternalOutput") if debug >= 3 else None
    dxn2 = nc.dram_tensor("dxn2", [NT, 128, 1024], BF16, kind="ExternalOutput") if debug >= 3 else None
    dwco = nc.dram_tensor("dwco", [NE, 128, 1024], BF16, kind="ExternalOutput") if debug >= 3 else None

    ctx = ExitStack()
    with ctx:
        ONES = ctx.enter_context(nc.sbuf_tensor("ONES", [128, 128], BF16))
        EYE = ctx.enter_context(nc.sbuf_tensor("EYE", [128, 128], BF16))
        GT = ctx.enter_context(nc.sbuf_tensor("GT", [128, NT], F32))
        BT = ctx.enter_context(nc.sbuf_tensor("BT", [128, NT], F32))
        BQS = ctx.enter_context(nc.sbuf_tensor("BQS", [128, NH], F32))
        BK = ctx.enter_context(nc.sbuf_tensor("BK", [128, NH], F32))
        BV = ctx.enter_context(nc.sbuf_tensor("BV", [128, NH], F32))
        BO = ctx.enter_context(nc.sbuf_tensor("BO", [128, NT], F32))
        MSKA = ctx.enter_context(nc.sbuf_tensor("MSKA", [128, 16], F32))
        MSKB = ctx.enter_context(nc.sbuf_tensor("MSKB", [128, 32], F32))
        RWT = ctx.enter_context(nc.sbuf_tensor("RWT", [128, NT * NE], BF16))
        KMXT = ctx.enter_context(nc.sbuf_tensor("KMXT", [128, NH], F32))
        QMXT = ctx.enter_context(nc.sbuf_tensor("QMXT", [128, NH], F32))
        NBH = ctx.enter_context(nc.sbuf_tensor("NBH", [128, NH], F32))
        TKN = ctx.enter_context(nc.sbuf_tensor("TKN", [128, 1], F32))

        block = ctx.enter_context(nc.Block())
        P = Prog(ctx, nc)
        gq = Ctr(ctx, nc, "c_gq")
        SY, GP, T, SC, V = "sync", "gpsimd", "tensor", "scalar", "vector"
        cS, cG, cT, cC, cV = (P.c[e] for e in ENGS)

        for dst, src in ((GT, lngb[0]), (BT, lngb[1]), (BQS, bqkv[0]), (BK, bqkv[1]),
                         (BV, bqkv[2]), (BO, boT), (MSKA, mska), (MSKB, mskb),
                         (EYE, eye)):
            P.dma(SY, dst[:, :], src)
        P.op(GP, lambda e: e.memset(ONES[:, :], 1.0), ctr=gq)
        P.op(GP, lambda e: e.memset(KMXT[:, :], 0.0), ctr=gq)
        P.op(GP, lambda e: e.memset(QMXT[:, :], 0.0), ctr=gq)
        P.dma(GP, RWT[:, :].rearrange("p (t c) -> p t c", t=NT), rw)
        p0_sy, p0_gp = cS.n, cG.n
        P.wait(V, cS, p0_sy)
        P.op(V, lambda e: e.tensor_scalar_mul(BQS[:, :], BQS[:, :], ISQ))

        with ExitStack() as p1:
            XCB = p1.enter_context(nc.sbuf_tensor("XCB", [128, NT * 512], BF16))
            XS1 = p1.enter_context(nc.sbuf_tensor("XS1", [128, NT * 512], BF16))
            XNH = [p1.enter_context(nc.sbuf_tensor(f"XNH{i}", [128, NT * 2048], BF16)) for i in range(2)]
            MUB = p1.enter_context(nc.sbuf_tensor("MUB", [128, 512], F32))
            VARB = p1.enter_context(nc.sbuf_tensor("VARB", [128, 512], F32))
            MUBB = p1.enter_context(nc.sbuf_tensor("MUBB", [128, 512], BF16))
            RBB = p1.enter_context(nc.sbuf_tensor("RBB", [128, 512], BF16))
            WKH = [p1.enter_context(nc.sbuf_tensor(f"WKH{i}", [128, NT * 128], BF16)) for i in range(2)]
            WVH = [p1.enter_context(nc.sbuf_tensor(f"WVH{i}", [128, NT * 128], BF16)) for i in range(2)]
            WQH = [p1.enter_context(nc.sbuf_tensor(f"WQH{i}", [128, NT * 128], BF16)) for i in range(2)]
            KST = [p1.enter_context(nc.sbuf_tensor(f"KST{i}", [128, 2048], BF16)) for i in range(2)]
            VST = [p1.enter_context(nc.sbuf_tensor("VST0", [128, 2048], BF16))] * 2
            QST = [p1.enter_context(nc.sbuf_tensor(f"QST{i}", [128, 512], BF16)) for i in range(2)]
            pST = [p1.enter_context(nc.psum_tensor(f"pST{i}", [128, 512], F32)) for i in range(2)]
            pPR = [p1.enter_context(nc.psum_tensor(f"pPR{i}", [128, 512], F32)) for i in range(4)]
            pQP = p1.enter_context(nc.psum_tensor("pQP", [128, 512], F32))
            pBH = p1.enter_context(nc.psum_tensor("pBH", [128, 512], F32))

            xb_ld = {}
            xn_done = {}
            stats_freed = {}
            sq_done = {}
            ev_done = {}
            sp_done = {}
            tt1_done = {}

            def emit_ln_chunk(c):
                hpar = c // 4
                if c >= 1:
                    P.wait(GP, cV, tt1_done[c - 1])
                xb_ld[c] = P.dma(GP, XCB[:, :], xT[c])
                P.wait(SC, cG, xb_ld[c])
                if c >= 1:
                    P.wait(SC, cT)
                    P.selfwait(SC)
                sq_done[c] = P.op(SC, lambda e: e.activation(XS1[:, :], XCB[:, :], AF.Square))
                P.wait(T, cG, xb_ld[c])
                P.wait(T, cC, sq_done[c])
                P.wait(T, gq, 3)
                if c >= 1:
                    P.wait(T, cV, stats_freed[c - 1])
                def _stats(e):
                    for t in range(NT):
                        e.matmul(pST[0][:, :], ONES[:, :], XCB[:, t * 512:(t + 1) * 512],
                                 start=(t == 0), stop=(t == NT - 1))
                    for t in range(NT):
                        ins = e.matmul(pST[1][:, :], ONES[:, :], XS1[:, t * 512:(t + 1) * 512],
                                       start=(t == 0), stop=(t == NT - 1))
                    return ins
                st_done = P.op(T, _stats)
                P.wait(V, cT, st_done)
                P.op(V, lambda e: e.tensor_scalar_mul(MUB[:, :], pST[0][:, :], RD))
                P.selfwait(V)
                P.op(V, lambda e: e.tensor_copy(MUBB[:, :], MUB[:, :]))
                P.selfwait(V)
                P.op(V, lambda e: e.tensor_mul(MUB[:, :], MUB[:, :], MUB[:, :]))
                P.selfwait(V)
                P.op(V, lambda e: e.scalar_tensor_tensor(
                    VARB[:, :], pST[1][:, :], RD, MUB[:, :], ALU.mult, ALU.subtract))
                stats_freed[c] = cV.n
                P.selfwait(V)
                mubb_done = P.op(V, lambda e: e.tensor_scalar(
                    VARB[:, :], VARB[:, :], 1.0 + EPS, EPS * EPS, ALU.mult, ALU.add))
                P.wait(SC, cV, mubb_done)
                P.op(SC, lambda e: e.activation(VARB[:, :], VARB[:, :], AF.Ln))
                P.selfwait(SC)
                rbb_done = P.op(SC, lambda e: e.activation(
                    RBB[:, :], VARB[:, :], AF.Exp, bias=0.0, scale=-0.5))
                P.wait(V, cC, rbb_done)
                def _tt1(e):
                    for t in range(NT):
                        ins = e.tensor_sub(XS1[:, t * 512:(t + 1) * 512],
                                           XCB[:, t * 512:(t + 1) * 512], MUBB[:, :])
                    return ins
                tt1_done[c] = P.op(V, _tt1)
                P.selfwait(V)
                def _tt2(e):
                    for t in range(NT):
                        ins = e.tensor_mul(XS1[:, t * 512:(t + 1) * 512],
                                           XS1[:, t * 512:(t + 1) * 512], RBB[:, :])
                    return ins
                tt2_done = P.op(V, _tt2)
                P.wait(SC, cV, tt2_done)
                hc = c % 4
                def _aff(e, hc=hc, hpar=hpar):
                    for t in range(NT):
                        ins = e.activation(XNH[hpar][:, t * 2048 + hc * 512: t * 2048 + (hc + 1) * 512],
                                           XS1[:, t * 512:(t + 1) * 512], AF.Identity,
                                           bias=BT[:, t:t + 1], scale=GT[:, t:t + 1])
                    return ins
                xn_done[c] = P.op(SC, _aff)

            def emit_proj_head(half, h):
                par = h % 2
                idx = half * NH + h
                if h == 0:
                    P.wait(GP, cC, xn_done[half * 4 + 3])
                if idx >= 2:
                    P.wait(GP, cT, head_mm_done[idx - 2])
                P.dma(GP, WKH[par][:, :], wk[h])
                P.dma(GP, WVH[par][:, :], wv[h])
                w_ld = P.dma(GP, WQH[par][:, :], wq[h])
                if idx >= 2:
                    P.wait(SC, cG, sp_done[idx - 2])
                if idx >= 1:
                    P.wait(SC, cG, sp_done[idx - 1])
                P.wait(T, cC, xn_done[half * 4 + 3])
                P.wait(T, cG, w_ld)
                for kind, wbuf, stg, pb0, bias in (
                        ("k", WKH, KST, 0, BK), ("v", WVH, VST, 2, BV)):
                    for n4 in range(4):
                        pb = pb0 + n4 % 2
                        if ev_done.get((idx, kind, n4 - 2)) is not None:
                            P.wait(T, cC, ev_done[(idx, kind, n4 - 2)])
                        elif ev_done.get((idx - 1, kind, n4 + 2)) is not None:
                            P.wait(T, cC, ev_done[(idx - 1, kind, n4 + 2)])
                        def _proj(e, par=par, pb=pb, n4=n4, wbuf=wbuf, half=half):
                            for t in range(NT):
                                ins = e.matmul(
                                    pPR[pb][:, :], wbuf[par][:, t * 128:(t + 1) * 128],
                                    XNH[half][:, t * 2048 + n4 * 512: t * 2048 + (n4 + 1) * 512],
                                    start=(t == 0), stop=(t == NT - 1))
                            return ins
                        mm = P.op(T, _proj)
                        P.wait(SC, cT, mm)
                        ev_done[(idx, kind, n4)] = P.op(
                            SC, lambda e, par=par, pb=pb, n4=n4, h=h, stg=stg, bias=bias:
                            e.activation(stg[par][:, n4 * 512:(n4 + 1) * 512], pPR[pb][:, :],
                                         AF.Identity, bias=bias[:, h:h + 1], scale=1.0))
                if idx >= 1:
                    P.wait(T, cC, ev_done[(idx - 1, "q", 0)])
                def _qproj(e, par=par, half=half):
                    for t in range(NT):
                        ins = e.matmul(pQP[:, :], WQH[par][:, t * 128:(t + 1) * 128],
                                       XNH[half][:, t * 2048: t * 2048 + 512],
                                       start=(t == 0), stop=(t == NT - 1))
                    return ins
                mm = P.op(T, _qproj)
                head_mm_done[idx] = mm
                P.wait(SC, cT, mm)
                ev_done[(idx, "q", 0)] = P.op(
                    SC, lambda e, par=par, h=h: e.activation(
                        QST[par][:, :], pQP[:, :], AF.Identity,
                        bias=BQS[:, h:h + 1], scale=ISQ))
                P.wait(GP, cC, ev_done[(idx, "k", 3)])
                P.dma(GP, kvK[h][:, half * 2048:(half + 1) * 2048], KST[par][:, :])
                P.wait(GP, cC, ev_done[(idx, "v", 3)])
                P.dma(GP, kvV[h][:, half * 2048:(half + 1) * 2048], VST[par][:, :])
                P.wait(GP, cC, ev_done[(idx, "q", 0)])
                sp_done[idx] = P.dma(
                    GP, kvQ[h][:, half * 512:(half + 1) * 512], QST[par][:, :])
                P.wait(SC, cG, sp_done[idx])
                P.selfwait(SC)
                P.op(SC, lambda e, par=par: e.activation(KST[par][:, :], KST[par][:, :], AF.Square))
                nsq = P.op(SC, lambda e, par=par: e.activation(QST[par][:, :], QST[par][:, :], AF.Square))
                P.wait(T, cC, nsq)
                for c4 in range(5):
                    P.wait(T, cV, cV.n)
                    def _nmm(e, c4=c4, par=par):
                        if c4 < 4:
                            return e.matmul(pBH[:, :], ONES[:, :],
                                            KST[par][:, c4 * 512:(c4 + 1) * 512], start=True, stop=True)
                        return e.matmul(pBH[:, :], ONES[:, :], QST[par][:, :], start=True, stop=True)
                    mm = P.op(T, _nmm)
                    P.wait(V, cT, mm)
                    P.op(V, lambda e: e.reduce_max(TKN[:, :], pBH[:, :], AX.X))
                    P.selfwait(V)
                    tgt = KMXT if c4 < 4 else QMXT
                    P.op(V, lambda e, tgt=tgt, h=h: e.tensor_max(
                        tgt[:, h:h + 1], tgt[:, h:h + 1], TKN[:, :]))
                    P.selfwait(V)

            head_mm_done = {}
            for c in range(4):
                emit_ln_chunk(c)
            for h in range(NH):
                emit_proj_head(0, h)
                if h in (1, 3, 5, 7):
                    emit_ln_chunk(4 + (h - 1) // 2)
            for h in range(NH):
                emit_proj_head(1, h)
            P.barrier()
        if phases <= 1:
            _tail_copyzero(P, nc, out)
            P.finalize(block)
            return nc

        with ExitStack() as pc_scope:
          X2F = pc_scope.enter_context(nc.sbuf_tensor("X2F", [128, NT * 1024], F32))
          with ExitStack() as pb_scope:
            AVN = pb_scope.enter_context(nc.sbuf_tensor("AVN", [128, NH * 1024], BF16))
            with ExitStack() as p2:
                KH = [p2.enter_context(nc.sbuf_tensor(f"KH{i}", [128, S], BF16)) for i in range(2)]
                VH = [p2.enter_context(nc.sbuf_tensor(f"VH{i}", [128, S], BF16)) for i in range(2)]
                VT = [p2.enter_context(nc.sbuf_tensor(f"VT{i}", [128, S], BF16)) for i in range(2)]
                QH = [p2.enter_context(nc.sbuf_tensor(f"QH{i}", [128, 1024], BF16)) for i in range(2)]
                BIA = [p2.enter_context(nc.sbuf_tensor(f"BIA{i}", [128, 16], F32)) for i in range(2)]
                BIB = [p2.enter_context(nc.sbuf_tensor(f"BIB{i}", [128, 32], F32)) for i in range(2)]
                EXS = [p2.enter_context(nc.sbuf_tensor(f"EXS{i}", [128, 16 * 512], BF16)) for i in range(2)]
                RDI = p2.enter_context(nc.sbuf_tensor("RDI", [128, 512], F32))
                RDG = p2.enter_context(nc.sbuf_tensor("RDG", [128, 4 * 512], BF16))
                pSW = [p2.enter_context(nc.psum_tensor(f"pSW{i}", [128, 1024], F32)) for i in range(2)]
                pAV = [p2.enter_context(nc.psum_tensor(f"pAV{i}", [128, 512], F32)) for i in range(2)]
                pDEN = [p2.enter_context(nc.psum_tensor(f"pDEN{i}", [128, 512], F32)) for i in range(2)]

                rdg_ld = P.dma(SY, RDG[:, :], rdg)
                P.wait(V, cS, rdg_ld)
                P.join(V)
                P.selfwait(V)
                P.op(V, lambda e: e.tensor_mul(NBH[:, :], QMXT[:, :], KMXT[:, :]))
                P.selfwait(V)
                nb_v = P.op(V, lambda e: e.tensor_scalar_add(NBH[:, :], NBH[:, :], 1e-30))
                P.wait(SC, cV, nb_v)
                P.op(SC, lambda e: e.activation(NBH[:, :], NBH[:, :], AF.Ln))
                P.selfwait(SC)
                nb_s = P.op(SC, lambda e: e.activation(NBH[:, :], NBH[:, :], AF.Exp, bias=0.0, scale=0.5))
                P.wait(V, cC, nb_s)
                nb_done = P.op(V, lambda e: e.tensor_scalar_mul(NBH[:, :], NBH[:, :], -1.0))

                def uinfo(u):
                    h, ui = divmod(u, 3)
                    slot = 0 if ui == 0 else 1
                    ktbase = 16 if ui == 2 else 0
                    diag = ui in (0, 2)
                    avstart = ui in (0, 1)
                    avstop = ui in (0, 2)
                    return h, ui, slot, ktbase, diag, avstart, avstop

                kv_ld = {}
                exp_done = {}
                tr_evac = {}
                dtt = {}
                last_slot_evac = {0: 0, 1: 0}
                last_rdi_evac = 0
                head_guard = {}

                def emit_loads(hh):
                    p = hh % 2
                    if hh >= 2:
                        P.wait(GP, cT)
                    kv_ld[hh] = max(P.dma(GP, KH[p][:, :], kvK[hh]),
                                    P.dma(GP, VH[p][:, :], kvV[hh]),
                                    P.dma(GP, QH[p][:, :], kvQ[hh]))

                bias_done = {}

                def emit_bias(hh):
                    p = hh % 2
                    P.wait(V, cV, nb_done)
                    if hh >= 2:
                        P.wait(V, cC, exp_done[(3 * (hh - 2) + 2, 15)])
                    P.op(V, lambda e, p=p, hh=hh: e.tensor_single_scalar(
                        BIA[p][:, :], MSKA[:, :], NBH[:, hh:hh + 1], ALU.add))
                    bias_done[hh] = P.op(V, lambda e, p=p, hh=hh: e.tensor_single_scalar(
                        BIB[p][:, :], MSKB[:, :], NBH[:, hh:hh + 1], ALU.add))

                def emit_trans(hh):
                    p = hh % 2
                    P.wait(T, cG, kv_ld[hh])
                    P.wait(T, cC, exp_done[(3 * hh, 15)])
                    for g8 in range(8):
                        tb, tc = g8 % 2, (g8 // 2) % 2
                        if (hh, g8 - 4) in tr_evac:
                            P.wait(T, cV, tr_evac[(hh, g8 - 4)])
                        def _trp(e, p=p, g8=g8, tb=tb, tc=tc):
                            for i4 in range(4):
                                kt = g8 * 4 + i4
                                ins = e.matmul(pSW[tb][:, tc * 512 + i4 * 128: tc * 512 + (i4 + 1) * 128],
                                               VH[p][:, kt * 128:(kt + 1) * 128],
                                               EYE[:, :], start=True, stop=True)
                            return ins
                        mm = P.op(T, _trp)
                        P.wait(V, cT, mm)
                        tr_evac[(hh, g8)] = P.op(V, lambda e, p=p, g8=g8, tb=tb, tc=tc: e.tensor_copy(
                            VT[p][:, g8 * 512:(g8 + 1) * 512], pSW[tb][:, tc * 512:(tc + 1) * 512]))

                def emit_diag(u):
                    _, _, _, _, diag, _, _ = uinfo(u)
                    if not diag:
                        return
                    upar = u % 2
                    P.wait(V, cC, exp_done[(u, 3)])
                    dtt[u] = P.op(V, lambda e, upar=upar: e.tensor_mul(
                        EXS[upar][:, 0:2048], EXS[upar][:, 0:2048], RDG[:, :]))

                def emit_block(u_s, u_av):
                    nonlocal last_rdi_evac
                    if u_av is not None:
                        hA, uiA, slotA_, ktbA, diagA, avst, avsp = uinfo(u_av)
                        emit_diag(u_av)
                        P.wait(T, cC, exp_done[(u_av, 15)])
                        if diagA:
                            P.wait(T, cV, dtt[u_av])
                        if avst:
                            P.wait(T, cV, last_slot_evac[slotA_])
                        if uiA == 0:
                            P.wait(T, cV, tr_evac[(hA, 7)])
                    if u_s is not None:
                        hS, uiS, slotS_, ktbS, _, _, _ = uinfo(u_s)
                        sp = hS % 2
                        up = u_s % 2
                        if uiS == 0:
                            P.wait(T, cG, kv_ld[hS])
                            P.wait(SC, cV, bias_done[hS])
                        if uiS in (1, 2):
                            P.wait(T, cV, tr_evac[(hS, 7)])
                    for kt in range(16):
                        if u_s is not None:
                            if (u_s, kt - 2) in exp_done:
                                P.wait(T, cC, exp_done[(u_s, kt - 2)])
                            def _smm(e, sp=sp, up=up, kt=kt, ktbS=ktbS, slotS_=slotS_):
                                g = ktbS + kt
                                return e.matmul(pSW[up][:, (kt % 2) * 512:(kt % 2 + 1) * 512],
                                                KH[sp][:, g * 128:(g + 1) * 128],
                                                QH[sp][:, slotS_ * 512:(slotS_ + 1) * 512],
                                                start=True, stop=True)
                            mm = P.op(T, _smm)
                            P.wait(SC, cT, mm)
                            bias_t = (BIA[sp], kt) if uiS == 0 else (BIB[sp], ktbS + kt)
                            exp_done[(u_s, kt)] = P.op(
                                SC, lambda e, up=up, kt=kt, bt=bias_t:
                                e.activation(EXS[up][:, kt * 512:(kt + 1) * 512],
                                             pSW[up][:, (kt % 2) * 512:(kt % 2 + 1) * 512],
                                             AF.Exp, bias=bt[0][:, bt[1]:bt[1] + 1], scale=1.0))
                        if u_av is not None:
                            hA2, uiA2, slotA2, ktbA2, _, avst2, avsp2 = uinfo(u_av)
                            pA, uA = hA2 % 2, u_av % 2
                            def _avmm(e, pA=pA, uA=uA, slotA2=slotA2, ktbA2=ktbA2, kt=kt,
                                      avst2=avst2, avsp2=avsp2):
                                g = ktbA2 + kt
                                e.matmul(pAV[slotA2][:, :], VT[pA][:, g * 128:(g + 1) * 128],
                                         EXS[uA][:, kt * 512:(kt + 1) * 512],
                                         start=(avst2 and kt == 0), stop=(avsp2 and kt == 15))
                                ins = e.matmul(pDEN[slotA2][:, :], ONES[:, :],
                                               EXS[uA][:, kt * 512:(kt + 1) * 512],
                                               start=(avst2 and kt == 0), stop=(avsp2 and kt == 15))
                                return ins
                            avmm = P.op(T, _avmm)
                    if u_av is not None and uinfo(u_av)[6]:
                        hA, _, slotA_, _, _, _, _ = uinfo(u_av)
                        P.wait(SC, cT, avmm)
                        P.wait(SC, cV, last_rdi_evac)
                        P.op(SC, lambda e, s=slotA_: e.activation(RDI[:, :], pDEN[s][:, :], AF.Ln))
                        P.selfwait(SC)
                        rdi_s = P.op(SC, lambda e: e.activation(RDI[:, :], RDI[:, :], AF.Exp,
                                                                bias=0.0, scale=-1.0))
                        P.wait(V, cC, rdi_s)
                        last_slot_evac[slotA_] = P.op(
                            V, lambda e, hA=hA, s=slotA_: e.tensor_mul(
                                AVN[:, hA * 1024 + s * 512: hA * 1024 + (s + 1) * 512],
                                pAV[s][:, :], RDI[:, :]))
                        last_rdi_evac = last_slot_evac[slotA_]

                emit_loads(0)
                emit_bias(0)
                for h in range(NH):
                    emit_block(3 * h, 3 * h - 1 if h > 0 else None)
                    if h + 1 < NH:
                        emit_loads(h + 1)
                        emit_bias(h + 1)
                    emit_trans(h)
                    emit_block(3 * h + 1, 3 * h)
                    emit_block(3 * h + 2, 3 * h + 1)
                emit_block(None, 47)
                P.barrier()
            if debug >= 2:
                for h in range(NH):
                    P.dma(SY, davn[h], AVN[:, h * 1024:(h + 1) * 1024], ctr=P.so)
                P.wait(SY, P.so)
            if phases <= 2:
                _tail_copyzero(P, nc, out)
                P.finalize(block)
                return nc

            with ExitStack() as p3:
                WOS = [p3.enter_context(nc.sbuf_tensor(f"WOS{i}", [128, NH * 128], BF16)) for i in range(2)]
                pO = [p3.enter_context(nc.psum_tensor(f"pO{i}", [128, 512], F32)) for i in range(2)]
                xres_ld = 0
                for sl, creg in ((0, 0), (1, 4)):
                    xres_ld = P.dma(SY, X2F[:, sl * 8192:(sl + 1) * 8192], xT[creg])
                wo_ld = {}
                oev = {}
                for dout in range(NT):
                    par = dout % 2
                    if dout >= 2:
                        P.wait(GP, cT, wo_ld[dout - 2][1])
                    wo_ld[dout] = (P.dma(GP, WOS[par][:, :], wo[dout]), cT.n)
                    P.wait(T, cG, wo_ld[dout][0])
                    for n2 in range(2):
                        if dout >= 1:
                            P.wait(T, cV, oev[(dout - 1, n2)])
                        def _omm(e, par=par, n2=n2):
                            for ht in range(NH):
                                ins = e.matmul(pO[n2][:, :], WOS[par][:, ht * 128:(ht + 1) * 128],
                                               AVN[:, ht * 1024 + n2 * 512: ht * 1024 + (n2 + 1) * 512],
                                               start=(ht == 0), stop=(ht == NH - 1))
                            return ins
                        mm = P.op(T, _omm)
                        wo_ld[dout] = (wo_ld[dout][0], cT.n)
                        P.wait(V, cT, mm)
                        if dout == 0 and n2 == 0:
                            P.wait(V, cS, xres_ld)
                        oev[(dout, n2)] = P.op(V, lambda e, dout=dout, n2=n2: e.scalar_tensor_tensor(
                            X2F[:, n2 * 8192 + dout * 512: n2 * 8192 + (dout + 1) * 512],
                            pO[n2][:, :], BO[:, dout:dout + 1],
                            X2F[:, n2 * 8192 + dout * 512: n2 * 8192 + (dout + 1) * 512],
                            ALU.add, ALU.add))
                P.barrier()
          if phases <= 3:
            if debug >= 3:
                for t in range(NT):
                    P.dma(SY, dx2[t], X2F[:, t * 1024:(t + 1) * 1024], ctr=P.so)
                P.wait(SY, P.so)
            _tail_copyzero(P, nc, out)
            P.finalize(block)
            return nc

          with ExitStack() as pc2:
            XN2 = pc2.enter_context(nc.sbuf_tensor("XN2", [128, NT * 1024], BF16))
            WB = pc2.enter_context(nc.sbuf_tensor("WB", [128, NE * 1024], BF16))
            with ExitStack() as pln:
                X2B = pln.enter_context(nc.sbuf_tensor("X2B", [128, NT * 1024], BF16))
                X2S = pln.enter_context(nc.sbuf_tensor("X2S", [128, NT * 512], BF16))
                MU3 = pln.enter_context(nc.sbuf_tensor("MU3", [128, 512], F32))
                M23 = pln.enter_context(nc.sbuf_tensor("M23", [128, 512], F32))
                VA3 = pln.enter_context(nc.sbuf_tensor("VA3", [128, 512], F32))
                MUB3 = pln.enter_context(nc.sbuf_tensor("MUB3", [128, 512], BF16))
                RBB3 = pln.enter_context(nc.sbuf_tensor("RBB3", [128, 512], BF16))
                SCT = pln.enter_context(nc.sbuf_tensor("SCT", [8, 1024], F32))
                SCK = pln.enter_context(nc.sbuf_tensor("SCK", [128, 64], F32))
                NM = pln.enter_context(nc.sbuf_tensor("NM", [128, 8], F32))
                EXR = pln.enter_context(nc.sbuf_tensor("EXR", [128, 64], F32))
                SEL = pln.enter_context(nc.sbuf_tensor("SEL", [128, 64], F32))
                WM = pln.enter_context(nc.sbuf_tensor("WM", [128, 64], F32))
                M2R = pln.enter_context(nc.sbuf_tensor("M2R", [128, 8], F32))
                DS = pln.enter_context(nc.sbuf_tensor("DS", [128, 8], F32))
                DIV = pln.enter_context(nc.sbuf_tensor("DIV", [128, 8], F32))
                WTOK = pln.enter_context(nc.sbuf_tensor("WTOK", [128, 64], F32))
                WS = pln.enter_context(nc.sbuf_tensor("WS", [8, 1024], F32))
                WROW = pln.enter_context(nc.sbuf_tensor("WROW", [1, 1024], F32))
                EYEF = pln.enter_context(nc.sbuf_tensor("EYEF", [128, 128], F32))
                ONESF = pln.enter_context(nc.sbuf_tensor("ONESF", [1, 128], F32))
                pST2 = [pln.enter_context(nc.psum_tensor(f"pST2{i}", [128, 512], F32)) for i in range(2)]
                pRT = pln.enter_context(nc.psum_tensor("pRT", [128, 512], F32))
                pTT = pln.enter_context(nc.psum_tensor("pTT", [128, 512], F32))
                pWB = pln.enter_context(nc.psum_tensor("pWB", [128, 1024], F32))

                P.dma(SY, EYEF[:, :], eyef)
                P.op(GP, lambda e: e.memset(ONESF[:, :], 1.0), ctr=gq)
                x2b_c = P.op(SC, lambda e: e.activation(X2B[:, :], X2F[:, :], AF.Identity))
                ln2 = {}
                for n2 in range(2):
                    P.wait(SC, cC, x2b_c)
                    if n2 == 1:
                        P.wait(SC, cT)
                        P.wait(SC, cV)
                    sq = P.op(SC, lambda e, n2=n2: e.activation(
                        X2S[:, :], X2B[:, n2 * 8192:(n2 + 1) * 8192], AF.Square))
                    P.wait(T, cC, sq)
                    if n2 == 1:
                        P.wait(T, cV)
                    def _st2(e, n2=n2):
                        for t in range(NT):
                            e.matmul(pST2[0][:, :], ONES[:, :],
                                     X2B[:, n2 * 8192 + t * 512: n2 * 8192 + (t + 1) * 512],
                                     start=(t == 0), stop=(t == NT - 1))
                        for t in range(NT):
                            ins = e.matmul(pST2[1][:, :], ONES[:, :], X2S[:, t * 512:(t + 1) * 512],
                                           start=(t == 0), stop=(t == NT - 1))
                        return ins
                    mm = P.op(T, _st2)
                    P.wait(V, cT, mm)
                    P.op(V, lambda e: e.tensor_scalar_mul(MU3[:, :], pST2[0][:, :], RD))
                    P.selfwait(V)
                    P.op(V, lambda e: e.tensor_mul(M23[:, :], MU3[:, :], MU3[:, :]))
                    P.selfwait(V)
                    P.op(V, lambda e: e.scalar_tensor_tensor(VA3[:, :], pST2[1][:, :], RD, M23[:, :],
                                                             ALU.mult, ALU.subtract))
                    P.selfwait(V)
                    P.op(V, lambda e: e.tensor_scalar_add(VA3[:, :], VA3[:, :], EPS))
                    P.selfwait(V)
                    mub_c = P.op(V, lambda e: e.tensor_copy(MUB3[:, :], MU3[:, :]))
                    P.wait(SC, cV, mub_c)
                    P.op(SC, lambda e: e.activation(VA3[:, :], VA3[:, :], AF.Ln))
                    P.selfwait(SC)
                    rb_c = P.op(SC, lambda e: e.activation(RBB3[:, :], VA3[:, :], AF.Exp, bias=0.0, scale=-0.5))
                    P.wait(V, cC, rb_c)
                    def _tt1(e, n2=n2):
                        for t in range(NT):
                            ins = e.tensor_sub(X2S[:, t * 512:(t + 1) * 512],
                                               X2B[:, n2 * 8192 + t * 512: n2 * 8192 + (t + 1) * 512],
                                               MUB3[:, :])
                        return ins
                    P.op(V, _tt1)
                    P.selfwait(V)
                    def _tt2(e, n2=n2):
                        for t in range(NT):
                            ins = e.tensor_mul(XN2[:, n2 * 8192 + t * 512: n2 * 8192 + (t + 1) * 512],
                                               X2S[:, t * 512:(t + 1) * 512], RBB3[:, :])
                        return ins
                    ln2[n2] = P.op(V, _tt2)

                P.wait(T, cV, ln2[1])
                rt_ev = {}
                for n2 in range(2):
                    if n2 == 1:
                        P.wait(T, cC, rt_ev[0])
                    def _rt1(e, n2=n2):
                        for t in range(NT):
                            ins = e.matmul(pRT[0:8, 0:512], RWT[:, t * 8:(t + 1) * 8],
                                           XN2[:, n2 * 8192 + t * 512: n2 * 8192 + (t + 1) * 512],
                                           start=(t == 0), stop=(t == NT - 1))
                        return ins
                    mm = P.op(T, _rt1)
                    P.wait(SC, cT, mm)
                    rt_ev[n2] = P.op(SC, lambda e, n2=n2: e.activation(
                        SCT[:, n2 * 512:(n2 + 1) * 512], pRT[0:8, 0:512], AF.Identity))
                P.wait(T, cC, rt_ev[1])
                P.wait(T, cS)
                def _trt(e):
                    for blk in range(8):
                        ins = e.transpose(pTT[:, blk * 8:(blk + 1) * 8],
                                          SCT[:, blk * 128:(blk + 1) * 128], EYEF[0:8, 0:8])
                    return ins
                mm = P.op(T, _trt)
                P.wait(V, cT, mm)
                sck_c = P.op(V, lambda e: e.tensor_copy(SCK[:, :], pTT[:, 0:64]))
                P.selfwait(V)
                nm_c = 0
                for blk in range(8):
                    nm_c = P.op(V, lambda e, blk=blk: e.reduce_max(
                        NM[:, blk:blk + 1], SCK[:, blk * 8:(blk + 1) * 8], AX.X))
                P.selfwait(V)
                neg_c = P.op(V, lambda e: e.tensor_scalar_mul(NM[:, :], NM[:, :], -1.0))
                P.wait(SC, cV, neg_c)
                ex_c = 0
                for blk in range(8):
                    ex_c = P.op(SC, lambda e, blk=blk: e.activation(
                        EXR[:, blk * 8:(blk + 1) * 8], SCK[:, blk * 8:(blk + 1) * 8],
                        AF.Exp, bias=NM[:, blk:blk + 1], scale=1.0))
                P.wait(V, cC, ex_c)
                P.op(V, lambda e: e.tensor_single_scalar(SEL[:, :], EXR[:, :], 1.0, ALU.is_ge))
                P.selfwait(V)
                P.op(V, lambda e: e.scalar_tensor_tensor(WM[:, :], SEL[:, :], -1e9, EXR[:, :],
                                                         ALU.mult, ALU.add))
                P.selfwait(V)
                for blk in range(8):
                    P.op(V, lambda e, blk=blk: e.reduce_max(
                        M2R[:, blk:blk + 1], WM[:, blk * 8:(blk + 1) * 8], AX.X))
                P.selfwait(V)
                for blk in range(8):
                    P.op(V, lambda e, blk=blk: e.tensor_single_scalar(
                        SEL[:, blk * 8:(blk + 1) * 8], EXR[:, blk * 8:(blk + 1) * 8],
                        M2R[:, blk:blk + 1], ALU.is_ge))
                P.selfwait(V)
                P.op(V, lambda e: e.tensor_mul(WTOK[:, :], EXR[:, :], SEL[:, :]))
                P.selfwait(V)
                for blk in range(8):
                    P.op(V, lambda e, blk=blk: e.reduce_sum(
                        DS[:, blk:blk + 1], WTOK[:, blk * 8:(blk + 1) * 8], AX.X))
                P.selfwait(V)
                P.op(V, lambda e: e.reciprocal(DIV[:, :], DS[:, :]))
                P.selfwait(V)
                wt_c = 0
                for blk in range(8):
                    wt_c = P.op(V, lambda e, blk=blk: e.tensor_single_scalar(
                        WTOK[:, blk * 8:(blk + 1) * 8], WTOK[:, blk * 8:(blk + 1) * 8],
                        DIV[:, blk:blk + 1], ALU.mult))
                P.wait(T, cV, wt_c)
                def _trb(e):
                    for blk in range(8):
                        ins = e.transpose(pWB[0:8, blk * 128:(blk + 1) * 128],
                                          WTOK[:, blk * 8:(blk + 1) * 8], EYEF[:, :])
                    return ins
                mm = P.op(T, _trb)
                P.wait(SC, cT, mm)
                ws_c = P.op(SC, lambda e: e.activation(WS[:, :], pWB[0:8, :], AF.Identity))
                P.wait(GP, cC, ws_c)
                P.wait(T, gq, 4)
                wb_ev = {}
                wb_mm = {}
                for ee in range(NE):
                    if ee >= 1:
                        P.wait(GP, cT, wb_mm[(ee - 1, 1)])
                    wr_ld = P.dma(GP, WROW[0:1, :], WS[ee:ee + 1, :])
                    P.wait(T, cG, wr_ld)
                    for n2 in range(2):
                        if (ee, n2) != (0, 0):
                            prev = (ee, 0) if n2 else (ee - 1, 1)
                            P.wait(T, cC, wb_ev[prev])
                        wb_mm[(ee, n2)] = P.op(T, lambda e, ee=ee, n2=n2: e.matmul(
                            pWB[:, 0:512], ONESF[:, :],
                            WROW[0:1, n2 * 512:(n2 + 1) * 512],
                            start=True, stop=True))
                        P.wait(SC, cT, wb_mm[(ee, n2)])
                        wb_ev[(ee, n2)] = P.op(SC, lambda e, ee=ee, n2=n2: e.activation(
                            WB[:, ee * 1024 + n2 * 512: ee * 1024 + (n2 + 1) * 512],
                            pWB[:, 0:512], AF.Identity))
                P.barrier()
            if debug >= 3:
                for t in range(NT):
                    P.dma(SY, dx2[t], X2F[:, t * 1024:(t + 1) * 1024], ctr=P.so)
                for t in range(NT):
                    P.dma(SY, dxn2[t], XN2[:, t * 1024:(t + 1) * 1024], ctr=P.so)
                for ee in range(NE):
                    P.dma(SY, dwco[ee], WB[:, ee * 1024:(ee + 1) * 1024], ctr=P.so)
                P.wait(SY, P.so)
            if phases <= 4:
                _tail_copyzero(P, nc, out)
                P.finalize(block)
                return nc

            with ExitStack() as p5:
                WGF = [p5.enter_context(nc.sbuf_tensor(f"WGF{i}", [128, NT * 128], BF16)) for i in range(2)]
                WUF = [p5.enter_context(nc.sbuf_tensor(f"WUF{i}", [128, NT * 128], BF16)) for i in range(2)]
                WDF = [p5.enter_context(nc.sbuf_tensor(f"WDF{i}", [128, 2 * 8 * 128], BF16)) for i in range(2)]
                GS = p5.enter_context(nc.sbuf_tensor("GS", [128, 1024], BF16))
                US = p5.enter_context(nc.sbuf_tensor("US", [128, 1024], BF16))
                ACTB = [p5.enter_context(nc.sbuf_tensor(f"ACTB{i}", [128, 8 * 1024], BF16)) for i in range(2)]
                pG = [p5.enter_context(nc.psum_tensor(f"pG{i}", [128, 512], F32)) for i in range(2)]
                pU = [p5.enter_context(nc.psum_tensor(f"pU{i}", [128, 512], F32)) for i in range(2)]
                pD = [p5.enter_context(nc.psum_tensor(f"pD{i}", [128, 512], F32)) for i in range(2)]

                wld = {}
                gate_ev = {}
                up_ev = {}
                act_tt = {}
                dn_ev = {}
                dn_mm_last = 0
                for ee in range(NE):
                    epar = ee % 2
                    for fm in range(8):
                        fpar = fm % 2
                        if (ee, fm) >= (0, 2):
                            key = (ee, fm - 2) if fm >= 2 else (ee - 1, 6 + fm)
                            P.wait(GP, cT, wld[key])
                        P.dma(GP, WGF[fpar][:, :], wg[ee][fm])
                        gl = P.dma(GP, WUF[fpar][:, :], wu[ee][fm])
                        P.wait(T, cG, gl)
                        if act_tt.get((ee - 2, fm)) is not None:
                            P.wait(T, cV, act_tt[(ee - 2, fm)])
                        for n2 in range(2):
                            if gate_ev.get((ee, fm - 1, n2)) is not None:
                                P.wait(T, cC, gate_ev[(ee, fm - 1, n2)])
                            elif ee > 0 and fm == 0:
                                P.wait(T, cC, gate_ev[(ee - 1, 7, n2)])
                            def _gmm(e, fpar=fpar, n2=n2):
                                for t in range(NT):
                                    ins = e.matmul(pG[n2][:, :], WGF[fpar][:, t * 128:(t + 1) * 128],
                                                   XN2[:, n2 * 8192 + t * 512: n2 * 8192 + (t + 1) * 512],
                                                   start=(t == 0), stop=(t == NT - 1))
                                return ins
                            mm = P.op(T, _gmm)
                            P.wait(SC, cT, mm)
                            if n2 == 0:
                                pa = act_tt.get((ee, fm - 1), act_tt.get((ee - 1, 7)))
                                if pa is not None:
                                    P.wait(SC, cV, pa)
                            gate_ev[(ee, fm, n2)] = P.op(SC, lambda e, n2=n2: e.activation(
                                GS[:, n2 * 512:(n2 + 1) * 512], pG[n2][:, :], AF.Silu))
                        for n2 in range(2):
                            if up_ev.get((ee, fm - 1, n2)) is not None:
                                P.wait(T, cV, up_ev[(ee, fm - 1, n2)])
                            elif ee > 0 and fm == 0:
                                P.wait(T, cV, up_ev[(ee - 1, 7, n2)])
                            def _umm(e, fpar=fpar, n2=n2):
                                for t in range(NT):
                                    ins = e.matmul(pU[n2][:, :], WUF[fpar][:, t * 128:(t + 1) * 128],
                                                   XN2[:, n2 * 8192 + t * 512: n2 * 8192 + (t + 1) * 512],
                                                   start=(t == 0), stop=(t == NT - 1))
                                return ins
                            mm = P.op(T, _umm)
                            P.wait(V, cT, mm)
                            up_ev[(ee, fm, n2)] = P.op(V, lambda e, ee=ee, n2=n2: e.tensor_mul(
                                US[:, n2 * 512:(n2 + 1) * 512], pU[n2][:, :],
                                WB[:, ee * 1024 + n2 * 512: ee * 1024 + (n2 + 1) * 512]))
                        P.selfwait(V)
                        P.wait(V, cC, gate_ev[(ee, fm, 1)])
                        if dn_mm_last and ee >= 2:
                            P.wait(V, cT, dn_mm_last)
                        act_tt[(ee, fm)] = P.op(V, lambda e, epar=epar, fm=fm: e.tensor_mul(
                            ACTB[epar][:, fm * 1024:(fm + 1) * 1024], GS[:, :], US[:, :]))
                        wld[(ee, fm)] = cT.n
                    if epar == 1:
                        P.wait(T, cV, act_tt[(ee, 7)])
                        P.wait(T, cV, act_tt[(ee - 1, 7)])
                        for dout in range(NT):
                            dpar = dout % 2
                            if dout >= 2:
                                P.wait(GP, cT, dn_ev.get((ee, dout - 2, "mm"), 0))
                            P.wait(GP, cV, act_tt[(ee, 7)] if dout == 0 else 0)
                            dl = P.dma(GP, WDF[dpar][:, 0:1024], wd[ee - 1][dout])
                            dl = P.dma(GP, WDF[dpar][:, 1024:2048], wd[ee][dout])
                            P.wait(T, cG, dl)
                            for n2 in range(2):
                                if dn_ev.get((ee, dout - 1, n2)) is not None:
                                    P.wait(T, cV, dn_ev[(ee, dout - 1, n2)])
                                elif ee >= 3 and dout == 0:
                                    P.wait(T, cV, dn_ev[(ee - 2, NT - 1, n2)])
                                def _dmm(e, dpar=dpar, n2=n2, ee=ee):
                                    for half_e in range(2):
                                        ep = ee - 1 + half_e
                                        for tf in range(8):
                                            ins = e.matmul(
                                                pD[n2][:, :],
                                                WDF[dpar][:, half_e * 1024 + tf * 128: half_e * 1024 + (tf + 1) * 128],
                                                ACTB[ep % 2][:, tf * 1024 + n2 * 512: tf * 1024 + (n2 + 1) * 512],
                                                start=(half_e == 0 and tf == 0),
                                                stop=(half_e == 1 and tf == 7))
                                    return ins
                                mm = P.op(T, _dmm)
                                dn_ev[(ee, dout, "mm")] = cT.n
                                dn_mm_last = cT.n
                                P.wait(V, cT, mm)
                                if dn_ev.get((ee - 2, dout, n2)) is not None:
                                    P.wait(V, cV, dn_ev[(ee - 2, dout, n2)])
                                dn_ev[(ee, dout, n2)] = P.op(V, lambda e, dout=dout, n2=n2: e.tensor_add(
                                    X2F[:, n2 * 8192 + dout * 512: n2 * 8192 + (dout + 1) * 512],
                                    X2F[:, n2 * 8192 + dout * 512: n2 * 8192 + (dout + 1) * 512],
                                    pD[n2][:, :]))
                P.barrier()
          P.join(SY)
          for sl in range(2):
            for t in range(NT):
                P.dma(SY, out[sl][t], X2F[:, sl * 8192 + t * 512: sl * 8192 + (t + 1) * 512], ctr=P.so)
          P.wait(SY, P.so)
        P.finalize(block)
    return nc


def _tail_noop(P, nc, out):
    _tail_copyzero(P, nc, out)


def _tail_copyzero(P, nc, out):
    with ExitStack() as s:
        Z = s.enter_context(nc.sbuf_tensor("Zf", [128, 512], F32))
        P.op("gpsimd", lambda e: e.memset(Z[:, :], 0.0), inc=False)
        P.join("sync")
        for sl in range(2):
            for t in range(NT):
                P.dma("sync", out[sl][t], Z[:, :], ctr=P.so)
        P.wait("sync", P.so)



def core_layout(j):
    own_a, own_b = j, 7 - j
    need = [c for c in range(8) if c <= 7 - j and c not in (own_a, own_b)]
    lt = [c for c in need if c < j]
    rest = [c for c in need if c >= j]
    a_extra = (lt + rest)[:3]
    b_extra = [c for c in need if c not in a_extra]
    assert len(b_extra) <= 3
    regions = [own_a] + a_extra + [None] * (3 - len(a_extra)) \
        + [own_b] + b_extra + [None] * (3 - len(b_extra))
    return regions


def host_inputs(raw, b, j):
    X = raw["L0_attn_X"]
    regions = core_layout(j)
    xT = np.zeros((8, 128, NT * 512), np.float32)
    for r, c in enumerate(regions):
        if c is not None:
            blk = X[b, c * 512:(c + 1) * 512, :].T
            xT[r] = blk.reshape(NT, 128, 512).transpose(1, 0, 2).reshape(128, NT * 512)

    def heads_w(W):
        return np.ascontiguousarray(
            W.reshape(NT, 128, NH, 128).transpose(2, 1, 0, 3).reshape(NH, 128, NT * 128))

    maska = np.zeros((128, 16), np.float32)
    maskb = np.zeros((128, 32), np.float32)
    for t in range(16):
        r = t // 4
        c = regions[r]
        if r == 0:
            v = 0.0
        else:
            v = 0.0 if (c is not None and c < j) else NEG
        maska[:, t] = v
    for t in range(32):
        r = t // 4
        c = regions[r]
        if r == 4:
            v = 0.0
        else:
            v = 0.0 if (c is not None and c < 7 - j) else NEG
        maskb[:, t] = v

    rd = (np.arange(512)[:, None] <= np.arange(512)[None, :]).astype(np.float32)
    rdg = np.ascontiguousarray(
        np.concatenate([rd[t * 128:(t + 1) * 128] for t in range(4)], axis=1)).astype(bf)

    wo_r = raw["L0_attn_Wo"].reshape(NH, 128, NT, 128).transpose(2, 1, 0, 3)
    d = {
        "xT": xT,
        "wq": heads_w(raw["L0_attn_Wq"]),
        "wk": heads_w(raw["L0_attn_Wk"]),
        "wv": heads_w(raw["L0_attn_Wv"]),
        "bqkv": np.ascontiguousarray(np.stack([
            raw["L0_attn_bq"].reshape(NH, 128).T,
            raw["L0_attn_bk"].reshape(NH, 128).T,
            raw["L0_attn_bv"].reshape(NH, 128).T])),
        "lngb": np.ascontiguousarray(np.stack([
            raw["L0_attn_ln_g"].reshape(NT, 128).T,
            raw["L0_attn_ln_b"].reshape(NT, 128).T])),
        "wo": np.ascontiguousarray(wo_r.reshape(NT, 128, NH * 128)),
        "boT": np.ascontiguousarray(raw["L0_attn_bo"].reshape(NT, 128).T),
        "rw": np.ascontiguousarray(raw["L0_ffn_router_w"].reshape(NT, 128, NE).transpose(1, 0, 2)),
        "wg": np.ascontiguousarray(raw["L0_ffn_gate_proj"].reshape(
            NE, NT, 128, 8, 128).transpose(0, 3, 2, 1, 4).reshape(NE, 8, 128, NT * 128)),
        "wu": np.ascontiguousarray(raw["L0_ffn_up_proj"].reshape(
            NE, NT, 128, 8, 128).transpose(0, 3, 2, 1, 4).reshape(NE, 8, 128, NT * 128)),
        "wd": np.ascontiguousarray(raw["L0_ffn_down_proj"].reshape(
            NE, 8, 128, NT, 128).transpose(0, 3, 2, 1, 4).reshape(NE, NT, 128, 8 * 128)),
        "eye": np.eye(128, dtype=bf),
        "eyef": np.eye(128, dtype=np.float32),
        "rdg": rdg,
        "mska": maska,
        "mskb": maskb,
    }
    return d


_NC_CACHE = []


def kernel(**inputs):
    if not _NC_CACHE:
        _NC_CACHE.append(build_program())
    nc = _NC_CACHE[0]
    in_maps = [host_inputs(inputs, c // 4, c % 4) for c in range(8)]
    res = run_bass_kernel_spmd(nc, in_maps, core_ids=list(range(8)))
    out = np.zeros((2, 4096, 2048), np.float32)
    for core in range(8):
        b, j = core // 4, core % 4
        o = res.results[core]["out"]
        out[b, j * 512:(j + 1) * 512, :] = o[0].reshape(2048, 512).T
        out[b, (7 - j) * 512:(8 - j) * 512, :] = o[1].reshape(2048, 512).T
    return out


# version 09
# speedup vs baseline: 1.0794x; 1.0794x over previous
import math
import numpy as np
import ml_dtypes
from contextlib import ExitStack
import concourse.bass as bass
import concourse.mybir as mybir
from concourse.bass_utils import run_bass_kernel_spmd

F32, BF16 = mybir.dt.float32, mybir.dt.bfloat16
AF, ALU, AX = mybir.ActivationFunctionType, mybir.AluOpType, mybir.AxisListType

D, S, NH, DH, NT = 2048, 4096, 16, 128, 16
NE, FF = 8, 1024
EPS = 1e-5
ISQ = 1.0 / math.sqrt(128.0)
NEG = -1e9
RD = 1.0 / 2048.0
bf = ml_dtypes.bfloat16

ENGS = ("sync", "gpsimd", "tensor", "scalar", "vector")


class Ctr:
    def __init__(self, ctx, nc, name):
        self.h = ctx.enter_context(nc.semaphore(name))
        self.n = 0

    def bump(self, k=1):
        self.n += k
        return self.n


class Prog:

    def __init__(self, ctx, nc):
        self.nc = nc
        self.ops = {e: [] for e in ENGS}
        self.c = {e: Ctr(ctx, nc, "c_" + e) for e in ENGS}
        self.so = Ctr(ctx, nc, "so")

    def emit(self, eng, fn):
        self.ops[eng].append(fn)

    def wait(self, eng, ctr, val=None):
        t = ctr.n if val is None else val
        if t > 0:
            self.emit(eng, lambda e, _c=ctr.h, _t=t: e.wait_ge(_c, _t))

    def op(self, eng, fn, inc=True, k=1, ctr=None):
        c = ctr if ctr is not None else self.c[eng]
        if inc:
            self.emit(eng, lambda e, _f=fn, _c=c.h, _k=k: _f(e).then_inc(_c, _k))
            return c.bump(k)
        self.emit(eng, lambda e, _f=fn: _f(e))
        return c.n

    def dma(self, eng, out, in_, ctr=None, **kw):
        return self.op(eng, lambda e: e.dma_start(out, in_, **kw), k=16, ctr=ctr)

    def selfwait(self, eng):
        self.wait(eng, self.c[eng])

    def join(self, eng, others=None):
        for o in (others or ENGS):
            if o != eng:
                self.wait(eng, self.c[o])

    def barrier(self):
        for e in ENGS:
            self.join(e)

    def finalize(self, block):
        for eng in ENGS:
            todo = self.ops[eng]
            getattr(block, eng)(lambda e, _t=todo: [f(e) for f in _t])


def build_program(debug=0, phases=99):
    nc = bass.Bass("TRN2", target_bir_lowering=False)
    xT = nc.dram_tensor("xT", [8, 128, NT * 512], F32, kind="ExternalInput").ap()
    wq = nc.dram_tensor("wq", [NH, 128, NT * 128], F32, kind="ExternalInput").ap()
    wk = nc.dram_tensor("wk", [NH, 128, NT * 128], F32, kind="ExternalInput").ap()
    wv = nc.dram_tensor("wv", [NH, 128, NT * 128], F32, kind="ExternalInput").ap()
    bqkv = nc.dram_tensor("bqkv", [3, 128, NH], F32, kind="ExternalInput").ap()
    lngb = nc.dram_tensor("lngb", [2, 128, NT], F32, kind="ExternalInput").ap()
    wo = nc.dram_tensor("wo", [NT, 128, NH * 128], F32, kind="ExternalInput").ap()
    boT = nc.dram_tensor("boT", [128, NT], F32, kind="ExternalInput").ap()
    rw = nc.dram_tensor("rw", [128, NT, NE], F32, kind="ExternalInput").ap()
    wg = nc.dram_tensor("wg", [NE, 8, 128, NT * 128], F32, kind="ExternalInput").ap()
    wu = nc.dram_tensor("wu", [NE, 8, 128, NT * 128], F32, kind="ExternalInput").ap()
    wd = nc.dram_tensor("wd", [NE, NT, 128, 8 * 128], F32, kind="ExternalInput").ap()
    eye = nc.dram_tensor("eye", [128, 128], BF16, kind="ExternalInput").ap()
    eyef = nc.dram_tensor("eyef", [128, 128], F32, kind="ExternalInput").ap()
    rdg = nc.dram_tensor("rdg", [128, 4 * 512], BF16, kind="ExternalInput").ap()
    mska = nc.dram_tensor("mska", [128, 16], F32, kind="ExternalInput").ap()
    mskb = nc.dram_tensor("mskb", [128, 32], F32, kind="ExternalInput").ap()
    out = nc.dram_tensor("out", [2, NT, 128, 512], F32, kind="ExternalOutput").ap()
    kvK = nc.dram_tensor("kvK", [NH, 128, S], BF16,
                         kind="ExternalOutput" if debug >= 1 else "Internal").ap()
    kvV = nc.dram_tensor("kvV", [NH, 128, S], BF16,
                         kind="ExternalOutput" if debug >= 1 else "Internal").ap()
    kvQ = nc.dram_tensor("kvQ", [NH, 128, 1024], BF16,
                         kind="ExternalOutput" if debug >= 1 else "Internal").ap()
    davn = nc.dram_tensor("davn", [NH, 128, 1024], BF16,
                          kind="ExternalOutput") if debug >= 2 else None
    dx2 = nc.dram_tensor("dx2", [NT, 128, 1024], F32, kind="ExternalOutput") if debug >= 3 else None
    dxn2 = nc.dram_tensor("dxn2", [NT, 128, 1024], BF16, kind="ExternalOutput") if debug >= 3 else None
    dwco = nc.dram_tensor("dwco", [NE, 128, 1024], BF16, kind="ExternalOutput") if debug >= 3 else None

    ctx = ExitStack()
    with ctx:
        ONES = ctx.enter_context(nc.sbuf_tensor("ONES", [128, 128], BF16))
        EYE = ctx.enter_context(nc.sbuf_tensor("EYE", [128, 128], BF16))
        GT = ctx.enter_context(nc.sbuf_tensor("GT", [128, NT], F32))
        BT = ctx.enter_context(nc.sbuf_tensor("BT", [128, NT], F32))
        BQS = ctx.enter_context(nc.sbuf_tensor("BQS", [128, NH], F32))
        BK = ctx.enter_context(nc.sbuf_tensor("BK", [128, NH], F32))
        BV = ctx.enter_context(nc.sbuf_tensor("BV", [128, NH], F32))
        BO = ctx.enter_context(nc.sbuf_tensor("BO", [128, NT], F32))
        MSKA = ctx.enter_context(nc.sbuf_tensor("MSKA", [128, 16], F32))
        MSKB = ctx.enter_context(nc.sbuf_tensor("MSKB", [128, 32], F32))
        RWT = ctx.enter_context(nc.sbuf_tensor("RWT", [128, NT * NE], BF16))
        KMXT = ctx.enter_context(nc.sbuf_tensor("KMXT", [128, NH], F32))
        QMXT = ctx.enter_context(nc.sbuf_tensor("QMXT", [128, NH], F32))
        NBH = ctx.enter_context(nc.sbuf_tensor("NBH", [128, NH], F32))
        TKN = ctx.enter_context(nc.sbuf_tensor("TKN", [128, 1], F32))

        block = ctx.enter_context(nc.Block())
        P = Prog(ctx, nc)
        gq = Ctr(ctx, nc, "c_gq")
        SY, GP, T, SC, V = "sync", "gpsimd", "tensor", "scalar", "vector"
        cS, cG, cT, cC, cV = (P.c[e] for e in ENGS)

        for dst, src in ((GT, lngb[0]), (BT, lngb[1]), (BQS, bqkv[0]), (BK, bqkv[1]),
                         (BV, bqkv[2]), (BO, boT), (MSKA, mska), (MSKB, mskb),
                         (EYE, eye)):
            P.dma(SY, dst[:, :], src)
        P.op(GP, lambda e: e.memset(ONES[:, :], 1.0), ctr=gq)
        P.op(GP, lambda e: e.memset(KMXT[:, :], 0.0), ctr=gq)
        P.op(GP, lambda e: e.memset(QMXT[:, :], 0.0), ctr=gq)
        P.dma(GP, RWT[:, :].rearrange("p (t c) -> p t c", t=NT), rw)
        p0_sy, p0_gp = cS.n, cG.n
        P.wait(V, cS, p0_sy)
        P.op(V, lambda e: e.tensor_scalar_mul(BQS[:, :], BQS[:, :], ISQ))

        with ExitStack() as p1:
            XCB = p1.enter_context(nc.sbuf_tensor("XCB", [128, NT * 512], BF16))
            XS1 = p1.enter_context(nc.sbuf_tensor("XS1", [128, NT * 512], BF16))
            XNH = [p1.enter_context(nc.sbuf_tensor(f"XNH{i}", [128, NT * 2048], BF16)) for i in range(2)]
            MUB = p1.enter_context(nc.sbuf_tensor("MUB", [128, 512], F32))
            VARB = p1.enter_context(nc.sbuf_tensor("VARB", [128, 512], F32))
            MUBB = p1.enter_context(nc.sbuf_tensor("MUBB", [128, 512], BF16))
            RBB = p1.enter_context(nc.sbuf_tensor("RBB", [128, 512], BF16))
            WKH = [p1.enter_context(nc.sbuf_tensor(f"WKH{i}", [128, NT * 128], BF16)) for i in range(2)]
            WVH = [p1.enter_context(nc.sbuf_tensor(f"WVH{i}", [128, NT * 128], BF16)) for i in range(2)]
            WQH = [p1.enter_context(nc.sbuf_tensor(f"WQH{i}", [128, NT * 128], BF16)) for i in range(2)]
            KST = [p1.enter_context(nc.sbuf_tensor(f"KST{i}", [128, 2048], BF16)) for i in range(2)]
            VST = [p1.enter_context(nc.sbuf_tensor("VST0", [128, 2048], BF16))] * 2
            QST = [p1.enter_context(nc.sbuf_tensor(f"QST{i}", [128, 512], BF16)) for i in range(2)]
            pST = [p1.enter_context(nc.psum_tensor(f"pST{i}", [128, 512], F32)) for i in range(2)]
            pPR = [p1.enter_context(nc.psum_tensor(f"pPR{i}", [128, 512], F32)) for i in range(4)]
            pQP = p1.enter_context(nc.psum_tensor("pQP", [128, 512], F32))
            pBH = p1.enter_context(nc.psum_tensor("pBH", [128, 512], F32))

            xb_ld = {}
            norm_pqp_done = {}
            xn_done = {}
            stats_freed = {}
            sq_done = {}
            ev_done = {}
            sp_done = {}
            tt1_done = {}

            def emit_ln_chunk(c):
                hpar = c // 4
                if c >= 1:
                    P.wait(GP, cV, tt1_done[c - 1])
                xb_ld[c] = P.dma(GP, XCB[:, :], xT[c])
                P.wait(SC, cG, xb_ld[c])
                if c >= 1:
                    P.wait(SC, cT)
                    P.selfwait(SC)
                sq_done[c] = P.op(SC, lambda e: e.activation(XS1[:, :], XCB[:, :], AF.Square))
                P.wait(T, cG, xb_ld[c])
                P.wait(T, cC, sq_done[c])
                P.wait(T, gq, 3)
                if c >= 1:
                    P.wait(T, cV, stats_freed[c - 1])
                def _stats(e):
                    for t in range(NT):
                        e.matmul(pST[0][:, :], ONES[:, :], XCB[:, t * 512:(t + 1) * 512],
                                 start=(t == 0), stop=(t == NT - 1))
                    for t in range(NT):
                        ins = e.matmul(pST[1][:, :], ONES[:, :], XS1[:, t * 512:(t + 1) * 512],
                                       start=(t == 0), stop=(t == NT - 1))
                    return ins
                st_done = P.op(T, _stats)
                P.wait(V, cT, st_done)
                P.op(V, lambda e: e.tensor_scalar_mul(MUB[:, :], pST[0][:, :], RD))
                P.selfwait(V)
                P.op(V, lambda e: e.tensor_copy(MUBB[:, :], MUB[:, :]))
                P.selfwait(V)
                P.op(V, lambda e: e.tensor_mul(MUB[:, :], MUB[:, :], MUB[:, :]))
                P.selfwait(V)
                P.op(V, lambda e: e.scalar_tensor_tensor(
                    VARB[:, :], pST[1][:, :], RD, MUB[:, :], ALU.mult, ALU.subtract))
                stats_freed[c] = cV.n
                P.selfwait(V)
                mubb_done = P.op(V, lambda e: e.tensor_scalar(
                    VARB[:, :], VARB[:, :], 1.0 + EPS, EPS * EPS, ALU.mult, ALU.add))
                P.wait(SC, cV, mubb_done)
                P.op(SC, lambda e: e.activation(VARB[:, :], VARB[:, :], AF.Ln))
                P.selfwait(SC)
                rbb_done = P.op(SC, lambda e: e.activation(
                    RBB[:, :], VARB[:, :], AF.Exp, bias=0.0, scale=-0.5))
                P.wait(V, cC, rbb_done)
                def _tt1(e):
                    for t in range(NT):
                        ins = e.tensor_sub(XS1[:, t * 512:(t + 1) * 512],
                                           XCB[:, t * 512:(t + 1) * 512], MUBB[:, :])
                    return ins
                tt1_done[c] = P.op(V, _tt1)
                P.selfwait(V)
                def _tt2(e):
                    for t in range(NT):
                        ins = e.tensor_mul(XS1[:, t * 512:(t + 1) * 512],
                                           XS1[:, t * 512:(t + 1) * 512], RBB[:, :])
                    return ins
                tt2_done = P.op(V, _tt2)
                P.wait(SC, cV, tt2_done)
                hc = c % 4
                def _aff(e, hc=hc, hpar=hpar):
                    for t in range(NT):
                        ins = e.activation(XNH[hpar][:, t * 2048 + hc * 512: t * 2048 + (hc + 1) * 512],
                                           XS1[:, t * 512:(t + 1) * 512], AF.Identity,
                                           bias=BT[:, t:t + 1], scale=GT[:, t:t + 1])
                    return ins
                xn_done[c] = P.op(SC, _aff)

            def emit_proj_head(half, h):
                par = h % 2
                idx = half * NH + h
                if h == 0:
                    P.wait(GP, cC, xn_done[half * 4 + 3])
                if idx >= 2:
                    P.wait(GP, cT, head_mm_done[idx - 2])
                P.dma(GP, WKH[par][:, :], wk[h])
                P.dma(GP, WVH[par][:, :], wv[h])
                w_ld = P.dma(GP, WQH[par][:, :], wq[h])
                if idx >= 2:
                    P.wait(SC, cG, sp_done[idx - 2])
                if idx >= 1:
                    P.wait(SC, cG, sp_done[idx - 1])
                P.wait(T, cC, xn_done[half * 4 + 3])
                P.wait(T, cG, w_ld)
                for kind, wbuf, stg, pb0, bias in (
                        ("k", WKH, KST, 0, BK), ("v", WVH, VST, 2, BV)):
                    for n4 in range(4):
                        pb = pb0 + n4 % 2
                        if ev_done.get((idx, kind, n4 - 2)) is not None:
                            P.wait(T, cC, ev_done[(idx, kind, n4 - 2)])
                        elif ev_done.get((idx - 1, kind, n4 + 2)) is not None:
                            P.wait(T, cC, ev_done[(idx - 1, kind, n4 + 2)])
                        def _proj(e, par=par, pb=pb, n4=n4, wbuf=wbuf, half=half):
                            for t in range(NT):
                                ins = e.matmul(
                                    pPR[pb][:, :], wbuf[par][:, t * 128:(t + 1) * 128],
                                    XNH[half][:, t * 2048 + n4 * 512: t * 2048 + (n4 + 1) * 512],
                                    start=(t == 0), stop=(t == NT - 1))
                            return ins
                        mm = P.op(T, _proj)
                        P.wait(SC, cT, mm)
                        ev_done[(idx, kind, n4)] = P.op(
                            SC, lambda e, par=par, pb=pb, n4=n4, h=h, stg=stg, bias=bias:
                            e.activation(stg[par][:, n4 * 512:(n4 + 1) * 512], pPR[pb][:, :],
                                         AF.Identity, bias=bias[:, h:h + 1], scale=1.0))
                if idx >= 1:
                    P.wait(T, cC, ev_done[(idx - 1, "q", 0)])
                    if (idx - 1) in norm_pqp_done:
                        P.wait(T, cV, norm_pqp_done[idx - 1])
                def _qproj(e, par=par, half=half):
                    for t in range(NT):
                        ins = e.matmul(pQP[:, :], WQH[par][:, t * 128:(t + 1) * 128],
                                       XNH[half][:, t * 2048: t * 2048 + 512],
                                       start=(t == 0), stop=(t == NT - 1))
                    return ins
                mm = P.op(T, _qproj)
                head_mm_done[idx] = mm
                P.wait(SC, cT, mm)
                ev_done[(idx, "q", 0)] = P.op(
                    SC, lambda e, par=par, h=h: e.activation(
                        QST[par][:, :], pQP[:, :], AF.Identity,
                        bias=BQS[:, h:h + 1], scale=ISQ))
                P.wait(GP, cC, ev_done[(idx, "k", 3)])
                P.dma(GP, kvK[h][:, half * 2048:(half + 1) * 2048], KST[par][:, :])
                P.wait(GP, cC, ev_done[(idx, "v", 3)])
                P.dma(GP, kvV[h][:, half * 2048:(half + 1) * 2048], VST[par][:, :])
                P.wait(GP, cC, ev_done[(idx, "q", 0)])
                sp_done[idx] = P.dma(
                    GP, kvQ[h][:, half * 512:(half + 1) * 512], QST[par][:, :])
                P.wait(SC, cG, sp_done[idx])
                P.selfwait(SC)
                P.op(SC, lambda e, par=par: e.activation(KST[par][:, :], KST[par][:, :], AF.Square))
                nsq = P.op(SC, lambda e, par=par: e.activation(QST[par][:, :], QST[par][:, :], AF.Square))
                P.wait(T, cC, nsq)
                nred = {}
                for c4 in range(5):
                    bank = pBH if c4 % 2 == 0 else pQP
                    if c4 >= 2:
                        P.wait(T, cV, nred[c4 - 2])
                    elif c4 == 0:
                        P.wait(T, cV, cV.n)
                    def _nmm(e, c4=c4, par=par, bank=bank):
                        if c4 < 4:
                            return e.matmul(bank[:, :], ONES[:, :],
                                            KST[par][:, c4 * 512:(c4 + 1) * 512], start=True, stop=True)
                        return e.matmul(bank[:, :], ONES[:, :], QST[par][:, :], start=True, stop=True)
                    mm = P.op(T, _nmm)
                    P.wait(V, cT, mm)
                    P.op(V, lambda e, bank=bank: e.reduce_max(TKN[:, :], bank[:, :], AX.X))
                    P.selfwait(V)
                    tgt = KMXT if c4 < 4 else QMXT
                    nred[c4] = P.op(V, lambda e, tgt=tgt, h=h: e.tensor_max(
                        tgt[:, h:h + 1], tgt[:, h:h + 1], TKN[:, :]))
                    P.selfwait(V)
                norm_pqp_done[idx] = nred[3]

            head_mm_done = {}
            for c in range(4):
                emit_ln_chunk(c)
            for h in range(NH):
                emit_proj_head(0, h)
                if h in (1, 3, 5, 7):
                    emit_ln_chunk(4 + (h - 1) // 2)
            for h in range(NH):
                emit_proj_head(1, h)
            P.barrier()
        if phases <= 1:
            _tail_copyzero(P, nc, out)
            P.finalize(block)
            return nc

        with ExitStack() as pc_scope:
          X2F = pc_scope.enter_context(nc.sbuf_tensor("X2F", [128, NT * 1024], F32))
          with ExitStack() as pb_scope:
            AVN = pb_scope.enter_context(nc.sbuf_tensor("AVN", [128, NH * 1024], BF16))
            with ExitStack() as p2:
                KH = [p2.enter_context(nc.sbuf_tensor(f"KH{i}", [128, S], BF16)) for i in range(2)]
                VH = [p2.enter_context(nc.sbuf_tensor(f"VH{i}", [128, S], BF16)) for i in range(2)]
                VT = [p2.enter_context(nc.sbuf_tensor(f"VT{i}", [128, S], BF16)) for i in range(2)]
                QH = [p2.enter_context(nc.sbuf_tensor(f"QH{i}", [128, 1024], BF16)) for i in range(2)]
                BIA = [p2.enter_context(nc.sbuf_tensor(f"BIA{i}", [128, 16], F32)) for i in range(2)]
                BIB = [p2.enter_context(nc.sbuf_tensor(f"BIB{i}", [128, 32], F32)) for i in range(2)]
                EXS = [p2.enter_context(nc.sbuf_tensor(f"EXS{i}", [128, 16 * 512], BF16)) for i in range(2)]
                RDI = p2.enter_context(nc.sbuf_tensor("RDI", [128, 512], F32))
                RDG = p2.enter_context(nc.sbuf_tensor("RDG", [128, 4 * 512], BF16))
                pSW = [p2.enter_context(nc.psum_tensor(f"pSW{i}", [128, 1024], F32)) for i in range(2)]
                pAV = [p2.enter_context(nc.psum_tensor(f"pAV{i}", [128, 512], F32)) for i in range(2)]
                pDEN = [p2.enter_context(nc.psum_tensor(f"pDEN{i}", [128, 512], F32)) for i in range(2)]

                rdg_ld = P.dma(SY, RDG[:, :], rdg)
                P.wait(V, cS, rdg_ld)
                P.join(V)
                P.selfwait(V)
                P.op(V, lambda e: e.tensor_mul(NBH[:, :], QMXT[:, :], KMXT[:, :]))
                P.selfwait(V)
                nb_v = P.op(V, lambda e: e.tensor_scalar_add(NBH[:, :], NBH[:, :], 1e-30))
                P.wait(SC, cV, nb_v)
                P.op(SC, lambda e: e.activation(NBH[:, :], NBH[:, :], AF.Ln))
                P.selfwait(SC)
                nb_s = P.op(SC, lambda e: e.activation(NBH[:, :], NBH[:, :], AF.Exp, bias=0.0, scale=0.5))
                P.wait(V, cC, nb_s)
                nb_done = P.op(V, lambda e: e.tensor_scalar_mul(NBH[:, :], NBH[:, :], -1.0))

                def uinfo(u):
                    h, ui = divmod(u, 3)
                    slot = 0 if ui == 0 else 1
                    ktbase = 16 if ui == 2 else 0
                    diag = ui in (0, 2)
                    avstart = ui in (0, 1)
                    avstop = ui in (0, 2)
                    return h, ui, slot, ktbase, diag, avstart, avstop

                kv_ld = {}
                exp_done = {}
                tr_evac = {}
                dtt = {}
                last_slot_evac = {0: 0, 1: 0}
                last_rdi_evac = 0
                head_guard = {}

                def emit_loads(hh):
                    p = hh % 2
                    if hh >= 2:
                        P.wait(GP, cT)
                    kv_ld[hh] = max(P.dma(GP, KH[p][:, :], kvK[hh]),
                                    P.dma(GP, VH[p][:, :], kvV[hh]),
                                    P.dma(GP, QH[p][:, :], kvQ[hh]))

                bias_done = {}

                def emit_bias(hh):
                    p = hh % 2
                    P.wait(V, cV, nb_done)
                    if hh >= 2:
                        P.wait(V, cC, exp_done[(3 * (hh - 2) + 2, 15)])
                    P.op(V, lambda e, p=p, hh=hh: e.tensor_single_scalar(
                        BIA[p][:, :], MSKA[:, :], NBH[:, hh:hh + 1], ALU.add))
                    bias_done[hh] = P.op(V, lambda e, p=p, hh=hh: e.tensor_single_scalar(
                        BIB[p][:, :], MSKB[:, :], NBH[:, hh:hh + 1], ALU.add))

                def emit_trans(hh):
                    p = hh % 2
                    P.wait(T, cG, kv_ld[hh])
                    P.wait(T, cC, exp_done[(3 * hh, 15)])
                    for g8 in range(8):
                        tb, tc = g8 % 2, (g8 // 2) % 2
                        if (hh, g8 - 4) in tr_evac:
                            P.wait(T, cV, tr_evac[(hh, g8 - 4)])
                        def _trp(e, p=p, g8=g8, tb=tb, tc=tc):
                            for i4 in range(4):
                                kt = g8 * 4 + i4
                                ins = e.matmul(pSW[tb][:, tc * 512 + i4 * 128: tc * 512 + (i4 + 1) * 128],
                                               VH[p][:, kt * 128:(kt + 1) * 128],
                                               EYE[:, :], start=True, stop=True)
                            return ins
                        mm = P.op(T, _trp)
                        P.wait(V, cT, mm)
                        tr_evac[(hh, g8)] = P.op(V, lambda e, p=p, g8=g8, tb=tb, tc=tc: e.tensor_copy(
                            VT[p][:, g8 * 512:(g8 + 1) * 512], pSW[tb][:, tc * 512:(tc + 1) * 512]))

                def emit_diag(u):
                    _, _, _, _, diag, _, _ = uinfo(u)
                    if not diag:
                        return
                    upar = u % 2
                    P.wait(V, cC, exp_done[(u, 3)])
                    dtt[u] = P.op(V, lambda e, upar=upar: e.tensor_mul(
                        EXS[upar][:, 0:2048], EXS[upar][:, 0:2048], RDG[:, :]))

                def emit_block(u_s, u_av):
                    nonlocal last_rdi_evac
                    if u_av is not None:
                        hA, uiA, slotA_, ktbA, diagA, avst, avsp = uinfo(u_av)
                        emit_diag(u_av)
                        P.wait(T, cC, exp_done[(u_av, 15)])
                        if diagA:
                            P.wait(T, cV, dtt[u_av])
                        if avst:
                            P.wait(T, cV, last_slot_evac[slotA_])
                        if uiA == 0:
                            P.wait(T, cV, tr_evac[(hA, 7)])
                    if u_s is not None:
                        hS, uiS, slotS_, ktbS, _, _, _ = uinfo(u_s)
                        sp = hS % 2
                        up = u_s % 2
                        if uiS == 0:
                            P.wait(T, cG, kv_ld[hS])
                            P.wait(SC, cV, bias_done[hS])
                        if uiS in (1, 2):
                            P.wait(T, cV, tr_evac[(hS, 7)])
                    for kt in range(16):
                        if u_s is not None:
                            if (u_s, kt - 2) in exp_done:
                                P.wait(T, cC, exp_done[(u_s, kt - 2)])
                            def _smm(e, sp=sp, up=up, kt=kt, ktbS=ktbS, slotS_=slotS_):
                                g = ktbS + kt
                                return e.matmul(pSW[up][:, (kt % 2) * 512:(kt % 2 + 1) * 512],
                                                KH[sp][:, g * 128:(g + 1) * 128],
                                                QH[sp][:, slotS_ * 512:(slotS_ + 1) * 512],
                                                start=True, stop=True)
                            mm = P.op(T, _smm)
                            P.wait(SC, cT, mm)
                            bias_t = (BIA[sp], kt) if uiS == 0 else (BIB[sp], ktbS + kt)
                            exp_done[(u_s, kt)] = P.op(
                                SC, lambda e, up=up, kt=kt, bt=bias_t:
                                e.activation(EXS[up][:, kt * 512:(kt + 1) * 512],
                                             pSW[up][:, (kt % 2) * 512:(kt % 2 + 1) * 512],
                                             AF.Exp, bias=bt[0][:, bt[1]:bt[1] + 1], scale=1.0))
                        if u_av is not None:
                            hA2, uiA2, slotA2, ktbA2, _, avst2, avsp2 = uinfo(u_av)
                            pA, uA = hA2 % 2, u_av % 2
                            def _avmm(e, pA=pA, uA=uA, slotA2=slotA2, ktbA2=ktbA2, kt=kt,
                                      avst2=avst2, avsp2=avsp2):
                                g = ktbA2 + kt
                                e.matmul(pAV[slotA2][:, :], VT[pA][:, g * 128:(g + 1) * 128],
                                         EXS[uA][:, kt * 512:(kt + 1) * 512],
                                         start=(avst2 and kt == 0), stop=(avsp2 and kt == 15))
                                ins = e.matmul(pDEN[slotA2][:, :], ONES[:, :],
                                               EXS[uA][:, kt * 512:(kt + 1) * 512],
                                               start=(avst2 and kt == 0), stop=(avsp2 and kt == 15))
                                return ins
                            avmm = P.op(T, _avmm)
                    if u_av is not None and uinfo(u_av)[6]:
                        hA, _, slotA_, _, _, _, _ = uinfo(u_av)
                        P.wait(SC, cT, avmm)
                        P.wait(SC, cV, last_rdi_evac)
                        P.op(SC, lambda e, s=slotA_: e.activation(RDI[:, :], pDEN[s][:, :], AF.Ln))
                        P.selfwait(SC)
                        rdi_s = P.op(SC, lambda e: e.activation(RDI[:, :], RDI[:, :], AF.Exp,
                                                                bias=0.0, scale=-1.0))
                        P.wait(V, cC, rdi_s)
                        last_slot_evac[slotA_] = P.op(
                            V, lambda e, hA=hA, s=slotA_: e.tensor_mul(
                                AVN[:, hA * 1024 + s * 512: hA * 1024 + (s + 1) * 512],
                                pAV[s][:, :], RDI[:, :]))
                        last_rdi_evac = last_slot_evac[slotA_]

                emit_loads(0)
                emit_bias(0)
                for h in range(NH):
                    emit_block(3 * h, 3 * h - 1 if h > 0 else None)
                    if h + 1 < NH:
                        emit_loads(h + 1)
                        emit_bias(h + 1)
                    emit_trans(h)
                    emit_block(3 * h + 1, 3 * h)
                    emit_block(3 * h + 2, 3 * h + 1)
                emit_block(None, 47)
                P.barrier()
            if debug >= 2:
                for h in range(NH):
                    P.dma(SY, davn[h], AVN[:, h * 1024:(h + 1) * 1024], ctr=P.so)
                P.wait(SY, P.so)
            if phases <= 2:
                _tail_copyzero(P, nc, out)
                P.finalize(block)
                return nc

            with ExitStack() as p3:
                WOS = [p3.enter_context(nc.sbuf_tensor(f"WOS{i}", [128, NH * 128], BF16)) for i in range(2)]
                pO = [p3.enter_context(nc.psum_tensor(f"pO{i}", [128, 512], F32)) for i in range(2)]
                xres_ld = 0
                for sl, creg in ((0, 0), (1, 4)):
                    xres_ld = P.dma(SY, X2F[:, sl * 8192:(sl + 1) * 8192], xT[creg])
                wo_ld = {}
                oev = {}
                for dout in range(NT):
                    par = dout % 2
                    if dout >= 2:
                        P.wait(GP, cT, wo_ld[dout - 2][1])
                    wo_ld[dout] = (P.dma(GP, WOS[par][:, :], wo[dout]), cT.n)
                    P.wait(T, cG, wo_ld[dout][0])
                    for n2 in range(2):
                        if dout >= 1:
                            P.wait(T, cV, oev[(dout - 1, n2)])
                        def _omm(e, par=par, n2=n2):
                            for ht in range(NH):
                                ins = e.matmul(pO[n2][:, :], WOS[par][:, ht * 128:(ht + 1) * 128],
                                               AVN[:, ht * 1024 + n2 * 512: ht * 1024 + (n2 + 1) * 512],
                                               start=(ht == 0), stop=(ht == NH - 1))
                            return ins
                        mm = P.op(T, _omm)
                        wo_ld[dout] = (wo_ld[dout][0], cT.n)
                        P.wait(V, cT, mm)
                        if dout == 0 and n2 == 0:
                            P.wait(V, cS, xres_ld)
                        oev[(dout, n2)] = P.op(V, lambda e, dout=dout, n2=n2: e.scalar_tensor_tensor(
                            X2F[:, n2 * 8192 + dout * 512: n2 * 8192 + (dout + 1) * 512],
                            pO[n2][:, :], BO[:, dout:dout + 1],
                            X2F[:, n2 * 8192 + dout * 512: n2 * 8192 + (dout + 1) * 512],
                            ALU.add, ALU.add))
                P.barrier()
          if phases <= 3:
            if debug >= 3:
                for t in range(NT):
                    P.dma(SY, dx2[t], X2F[:, t * 1024:(t + 1) * 1024], ctr=P.so)
                P.wait(SY, P.so)
            _tail_copyzero(P, nc, out)
            P.finalize(block)
            return nc

          with ExitStack() as pc2:
            XN2 = pc2.enter_context(nc.sbuf_tensor("XN2", [128, NT * 1024], BF16))
            WB = pc2.enter_context(nc.sbuf_tensor("WB", [128, NE * 1024], BF16))
            with ExitStack() as pln:
                X2B = pln.enter_context(nc.sbuf_tensor("X2B", [128, NT * 1024], BF16))
                X2S = pln.enter_context(nc.sbuf_tensor("X2S", [128, NT * 512], BF16))
                MU3 = pln.enter_context(nc.sbuf_tensor("MU3", [128, 512], F32))
                M23 = pln.enter_context(nc.sbuf_tensor("M23", [128, 512], F32))
                VA3 = pln.enter_context(nc.sbuf_tensor("VA3", [128, 512], F32))
                MUB3 = pln.enter_context(nc.sbuf_tensor("MUB3", [128, 512], BF16))
                RBB3 = pln.enter_context(nc.sbuf_tensor("RBB3", [128, 512], BF16))
                SCT = pln.enter_context(nc.sbuf_tensor("SCT", [8, 1024], F32))
                SCK = pln.enter_context(nc.sbuf_tensor("SCK", [128, 64], F32))
                NM = pln.enter_context(nc.sbuf_tensor("NM", [128, 8], F32))
                EXR = pln.enter_context(nc.sbuf_tensor("EXR", [128, 64], F32))
                SEL = pln.enter_context(nc.sbuf_tensor("SEL", [128, 64], F32))
                WM = pln.enter_context(nc.sbuf_tensor("WM", [128, 64], F32))
                M2R = pln.enter_context(nc.sbuf_tensor("M2R", [128, 8], F32))
                DS = pln.enter_context(nc.sbuf_tensor("DS", [128, 8], F32))
                DIV = pln.enter_context(nc.sbuf_tensor("DIV", [128, 8], F32))
                WTOK = pln.enter_context(nc.sbuf_tensor("WTOK", [128, 64], F32))
                WS = pln.enter_context(nc.sbuf_tensor("WS", [8, 1024], F32))
                WROW = pln.enter_context(nc.sbuf_tensor("WROW", [1, 1024], F32))
                EYEF = pln.enter_context(nc.sbuf_tensor("EYEF", [128, 128], F32))
                ONESF = pln.enter_context(nc.sbuf_tensor("ONESF", [1, 128], F32))
                pST2 = [pln.enter_context(nc.psum_tensor(f"pST2{i}", [128, 512], F32)) for i in range(2)]
                pRT = pln.enter_context(nc.psum_tensor("pRT", [128, 512], F32))
                pTT = pln.enter_context(nc.psum_tensor("pTT", [128, 512], F32))
                pWB = pln.enter_context(nc.psum_tensor("pWB", [128, 1024], F32))

                P.dma(SY, EYEF[:, :], eyef)
                P.op(GP, lambda e: e.memset(ONESF[:, :], 1.0), ctr=gq)
                x2b_c = P.op(SC, lambda e: e.activation(X2B[:, :], X2F[:, :], AF.Identity))
                ln2 = {}
                for n2 in range(2):
                    P.wait(SC, cC, x2b_c)
                    if n2 == 1:
                        P.wait(SC, cT)
                        P.wait(SC, cV)
                    sq = P.op(SC, lambda e, n2=n2: e.activation(
                        X2S[:, :], X2B[:, n2 * 8192:(n2 + 1) * 8192], AF.Square))
                    P.wait(T, cC, sq)
                    if n2 == 1:
                        P.wait(T, cV)
                    def _st2(e, n2=n2):
                        for t in range(NT):
                            e.matmul(pST2[0][:, :], ONES[:, :],
                                     X2B[:, n2 * 8192 + t * 512: n2 * 8192 + (t + 1) * 512],
                                     start=(t == 0), stop=(t == NT - 1))
                        for t in range(NT):
                            ins = e.matmul(pST2[1][:, :], ONES[:, :], X2S[:, t * 512:(t + 1) * 512],
                                           start=(t == 0), stop=(t == NT - 1))
                        return ins
                    mm = P.op(T, _st2)
                    P.wait(V, cT, mm)
                    P.op(V, lambda e: e.tensor_scalar_mul(MU3[:, :], pST2[0][:, :], RD))
                    P.selfwait(V)
                    P.op(V, lambda e: e.tensor_mul(M23[:, :], MU3[:, :], MU3[:, :]))
                    P.selfwait(V)
                    P.op(V, lambda e: e.scalar_tensor_tensor(VA3[:, :], pST2[1][:, :], RD, M23[:, :],
                                                             ALU.mult, ALU.subtract))
                    P.selfwait(V)
                    P.op(V, lambda e: e.tensor_scalar_add(VA3[:, :], VA3[:, :], EPS))
                    P.selfwait(V)
                    mub_c = P.op(V, lambda e: e.tensor_copy(MUB3[:, :], MU3[:, :]))
                    P.wait(SC, cV, mub_c)
                    P.op(SC, lambda e: e.activation(VA3[:, :], VA3[:, :], AF.Ln))
                    P.selfwait(SC)
                    rb_c = P.op(SC, lambda e: e.activation(RBB3[:, :], VA3[:, :], AF.Exp, bias=0.0, scale=-0.5))
                    P.wait(V, cC, rb_c)
                    def _tt1(e, n2=n2):
                        for t in range(NT):
                            ins = e.tensor_sub(X2S[:, t * 512:(t + 1) * 512],
                                               X2B[:, n2 * 8192 + t * 512: n2 * 8192 + (t + 1) * 512],
                                               MUB3[:, :])
                        return ins
                    P.op(V, _tt1)
                    P.selfwait(V)
                    def _tt2(e, n2=n2):
                        for t in range(NT):
                            ins = e.tensor_mul(XN2[:, n2 * 8192 + t * 512: n2 * 8192 + (t + 1) * 512],
                                               X2S[:, t * 512:(t + 1) * 512], RBB3[:, :])
                        return ins
                    ln2[n2] = P.op(V, _tt2)

                P.wait(T, cV, ln2[1])
                rt_ev = {}
                for n2 in range(2):
                    if n2 == 1:
                        P.wait(T, cC, rt_ev[0])
                    def _rt1(e, n2=n2):
                        for t in range(NT):
                            ins = e.matmul(pRT[0:8, 0:512], RWT[:, t * 8:(t + 1) * 8],
                                           XN2[:, n2 * 8192 + t * 512: n2 * 8192 + (t + 1) * 512],
                                           start=(t == 0), stop=(t == NT - 1))
                        return ins
                    mm = P.op(T, _rt1)
                    P.wait(SC, cT, mm)
                    rt_ev[n2] = P.op(SC, lambda e, n2=n2: e.activation(
                        SCT[:, n2 * 512:(n2 + 1) * 512], pRT[0:8, 0:512], AF.Identity))
                P.wait(T, cC, rt_ev[1])
                P.wait(T, cS)
                def _trt(e):
                    for blk in range(8):
                        ins = e.transpose(pTT[:, blk * 8:(blk + 1) * 8],
                                          SCT[:, blk * 128:(blk + 1) * 128], EYEF[0:8, 0:8])
                    return ins
                mm = P.op(T, _trt)
                P.wait(V, cT, mm)
                sck_c = P.op(V, lambda e: e.tensor_copy(SCK[:, :], pTT[:, 0:64]))
                P.selfwait(V)
                nm_c = 0
                for blk in range(8):
                    nm_c = P.op(V, lambda e, blk=blk: e.reduce_max(
                        NM[:, blk:blk + 1], SCK[:, blk * 8:(blk + 1) * 8], AX.X))
                P.selfwait(V)
                neg_c = P.op(V, lambda e: e.tensor_scalar_mul(NM[:, :], NM[:, :], -1.0))
                P.wait(SC, cV, neg_c)
                ex_c = 0
                for blk in range(8):
                    ex_c = P.op(SC, lambda e, blk=blk: e.activation(
                        EXR[:, blk * 8:(blk + 1) * 8], SCK[:, blk * 8:(blk + 1) * 8],
                        AF.Exp, bias=NM[:, blk:blk + 1], scale=1.0))
                P.wait(V, cC, ex_c)
                P.op(V, lambda e: e.tensor_single_scalar(SEL[:, :], EXR[:, :], 1.0, ALU.is_ge))
                P.selfwait(V)
                P.op(V, lambda e: e.scalar_tensor_tensor(WM[:, :], SEL[:, :], -1e9, EXR[:, :],
                                                         ALU.mult, ALU.add))
                P.selfwait(V)
                for blk in range(8):
                    P.op(V, lambda e, blk=blk: e.reduce_max(
                        M2R[:, blk:blk + 1], WM[:, blk * 8:(blk + 1) * 8], AX.X))
                P.selfwait(V)
                for blk in range(8):
                    P.op(V, lambda e, blk=blk: e.tensor_single_scalar(
                        SEL[:, blk * 8:(blk + 1) * 8], EXR[:, blk * 8:(blk + 1) * 8],
                        M2R[:, blk:blk + 1], ALU.is_ge))
                P.selfwait(V)
                P.op(V, lambda e: e.tensor_mul(WTOK[:, :], EXR[:, :], SEL[:, :]))
                P.selfwait(V)
                for blk in range(8):
                    P.op(V, lambda e, blk=blk: e.reduce_sum(
                        DS[:, blk:blk + 1], WTOK[:, blk * 8:(blk + 1) * 8], AX.X))
                P.selfwait(V)
                P.op(V, lambda e: e.reciprocal(DIV[:, :], DS[:, :]))
                P.selfwait(V)
                wt_c = 0
                for blk in range(8):
                    wt_c = P.op(V, lambda e, blk=blk: e.tensor_single_scalar(
                        WTOK[:, blk * 8:(blk + 1) * 8], WTOK[:, blk * 8:(blk + 1) * 8],
                        DIV[:, blk:blk + 1], ALU.mult))
                P.wait(T, cV, wt_c)
                def _trb(e):
                    for blk in range(8):
                        ins = e.transpose(pWB[0:8, blk * 128:(blk + 1) * 128],
                                          WTOK[:, blk * 8:(blk + 1) * 8], EYEF[:, :])
                    return ins
                mm = P.op(T, _trb)
                P.wait(SC, cT, mm)
                ws_c = P.op(SC, lambda e: e.activation(WS[:, :], pWB[0:8, :], AF.Identity))
                P.wait(GP, cC, ws_c)
                P.wait(T, gq, 4)
                wb_ev = {}
                wb_mm = {}
                for ee in range(NE):
                    if ee >= 1:
                        P.wait(GP, cT, wb_mm[(ee - 1, 1)])
                    wr_ld = P.dma(GP, WROW[0:1, :], WS[ee:ee + 1, :])
                    P.wait(T, cG, wr_ld)
                    for n2 in range(2):
                        if (ee, n2) != (0, 0):
                            prev = (ee, 0) if n2 else (ee - 1, 1)
                            P.wait(T, cC, wb_ev[prev])
                        wb_mm[(ee, n2)] = P.op(T, lambda e, ee=ee, n2=n2: e.matmul(
                            pWB[:, 0:512], ONESF[:, :],
                            WROW[0:1, n2 * 512:(n2 + 1) * 512],
                            start=True, stop=True))
                        P.wait(SC, cT, wb_mm[(ee, n2)])
                        wb_ev[(ee, n2)] = P.op(SC, lambda e, ee=ee, n2=n2: e.activation(
                            WB[:, ee * 1024 + n2 * 512: ee * 1024 + (n2 + 1) * 512],
                            pWB[:, 0:512], AF.Identity))
                P.barrier()
            if debug >= 3:
                for t in range(NT):
                    P.dma(SY, dx2[t], X2F[:, t * 1024:(t + 1) * 1024], ctr=P.so)
                for t in range(NT):
                    P.dma(SY, dxn2[t], XN2[:, t * 1024:(t + 1) * 1024], ctr=P.so)
                for ee in range(NE):
                    P.dma(SY, dwco[ee], WB[:, ee * 1024:(ee + 1) * 1024], ctr=P.so)
                P.wait(SY, P.so)
            if phases <= 4:
                _tail_copyzero(P, nc, out)
                P.finalize(block)
                return nc

            with ExitStack() as p5:
                WGF = [p5.enter_context(nc.sbuf_tensor(f"WGF{i}", [128, NT * 128], BF16)) for i in range(2)]
                WUF = [p5.enter_context(nc.sbuf_tensor(f"WUF{i}", [128, NT * 128], BF16)) for i in range(2)]
                WDF = [p5.enter_context(nc.sbuf_tensor(f"WDF{i}", [128, 2 * 8 * 128], BF16)) for i in range(2)]
                GS = p5.enter_context(nc.sbuf_tensor("GS", [128, 1024], BF16))
                US = p5.enter_context(nc.sbuf_tensor("US", [128, 1024], BF16))
                ACTB = [p5.enter_context(nc.sbuf_tensor(f"ACTB{i}", [128, 8 * 1024], BF16)) for i in range(2)]
                pG = [p5.enter_context(nc.psum_tensor(f"pG{i}", [128, 512], F32)) for i in range(2)]
                pU = [p5.enter_context(nc.psum_tensor(f"pU{i}", [128, 512], F32)) for i in range(2)]
                pD = [p5.enter_context(nc.psum_tensor(f"pD{i}", [128, 512], F32)) for i in range(2)]

                wld = {}
                gate_ev = {}
                up_ev = {}
                act_tt = {}
                dn_ev = {}
                dn_mm_last = 0
                for ee in range(NE):
                    epar = ee % 2
                    for fm in range(8):
                        fpar = fm % 2
                        if (ee, fm) >= (0, 2):
                            key = (ee, fm - 2) if fm >= 2 else (ee - 1, 6 + fm)
                            P.wait(GP, cT, wld[key])
                        P.dma(GP, WGF[fpar][:, :], wg[ee][fm])
                        gl = P.dma(GP, WUF[fpar][:, :], wu[ee][fm])
                        P.wait(T, cG, gl)
                        if act_tt.get((ee - 2, fm)) is not None:
                            P.wait(T, cV, act_tt[(ee - 2, fm)])
                        for n2 in range(2):
                            if gate_ev.get((ee, fm - 1, n2)) is not None:
                                P.wait(T, cC, gate_ev[(ee, fm - 1, n2)])
                            elif ee > 0 and fm == 0:
                                P.wait(T, cC, gate_ev[(ee - 1, 7, n2)])
                            def _gmm(e, fpar=fpar, n2=n2):
                                for t in range(NT):
                                    ins = e.matmul(pG[n2][:, :], WGF[fpar][:, t * 128:(t + 1) * 128],
                                                   XN2[:, n2 * 8192 + t * 512: n2 * 8192 + (t + 1) * 512],
                                                   start=(t == 0), stop=(t == NT - 1))
                                return ins
                            mm = P.op(T, _gmm)
                            P.wait(SC, cT, mm)
                            if n2 == 0:
                                pa = act_tt.get((ee, fm - 1), act_tt.get((ee - 1, 7)))
                                if pa is not None:
                                    P.wait(SC, cV, pa)
                            gate_ev[(ee, fm, n2)] = P.op(SC, lambda e, n2=n2: e.activation(
                                GS[:, n2 * 512:(n2 + 1) * 512], pG[n2][:, :], AF.Silu))
                        for n2 in range(2):
                            if up_ev.get((ee, fm - 1, n2)) is not None:
                                P.wait(T, cV, up_ev[(ee, fm - 1, n2)])
                            elif ee > 0 and fm == 0:
                                P.wait(T, cV, up_ev[(ee - 1, 7, n2)])
                            def _umm(e, fpar=fpar, n2=n2):
                                for t in range(NT):
                                    ins = e.matmul(pU[n2][:, :], WUF[fpar][:, t * 128:(t + 1) * 128],
                                                   XN2[:, n2 * 8192 + t * 512: n2 * 8192 + (t + 1) * 512],
                                                   start=(t == 0), stop=(t == NT - 1))
                                return ins
                            mm = P.op(T, _umm)
                            P.wait(V, cT, mm)
                            up_ev[(ee, fm, n2)] = P.op(V, lambda e, ee=ee, n2=n2: e.tensor_mul(
                                US[:, n2 * 512:(n2 + 1) * 512], pU[n2][:, :],
                                WB[:, ee * 1024 + n2 * 512: ee * 1024 + (n2 + 1) * 512]))
                        P.selfwait(V)
                        P.wait(V, cC, gate_ev[(ee, fm, 1)])
                        if dn_mm_last and ee >= 2:
                            P.wait(V, cT, dn_mm_last)
                        act_tt[(ee, fm)] = P.op(V, lambda e, epar=epar, fm=fm: e.tensor_mul(
                            ACTB[epar][:, fm * 1024:(fm + 1) * 1024], GS[:, :], US[:, :]))
                        wld[(ee, fm)] = cT.n
                    if epar == 1:
                        P.wait(T, cV, act_tt[(ee, 7)])
                        P.wait(T, cV, act_tt[(ee - 1, 7)])
                        for dout in range(NT):
                            dpar = dout % 2
                            if dout >= 2:
                                P.wait(GP, cT, dn_ev.get((ee, dout - 2, "mm"), 0))
                            P.wait(GP, cV, act_tt[(ee, 7)] if dout == 0 else 0)
                            dl = P.dma(GP, WDF[dpar][:, 0:1024], wd[ee - 1][dout])
                            dl = P.dma(GP, WDF[dpar][:, 1024:2048], wd[ee][dout])
                            P.wait(T, cG, dl)
                            for n2 in range(2):
                                if dn_ev.get((ee, dout - 1, n2)) is not None:
                                    P.wait(T, cV, dn_ev[(ee, dout - 1, n2)])
                                elif ee >= 3 and dout == 0:
                                    P.wait(T, cV, dn_ev[(ee - 2, NT - 1, n2)])
                                def _dmm(e, dpar=dpar, n2=n2, ee=ee):
                                    for half_e in range(2):
                                        ep = ee - 1 + half_e
                                        for tf in range(8):
                                            ins = e.matmul(
                                                pD[n2][:, :],
                                                WDF[dpar][:, half_e * 1024 + tf * 128: half_e * 1024 + (tf + 1) * 128],
                                                ACTB[ep % 2][:, tf * 1024 + n2 * 512: tf * 1024 + (n2 + 1) * 512],
                                                start=(half_e == 0 and tf == 0),
                                                stop=(half_e == 1 and tf == 7))
                                    return ins
                                mm = P.op(T, _dmm)
                                dn_ev[(ee, dout, "mm")] = cT.n
                                dn_mm_last = cT.n
                                P.wait(V, cT, mm)
                                if dn_ev.get((ee - 2, dout, n2)) is not None:
                                    P.wait(V, cV, dn_ev[(ee - 2, dout, n2)])
                                dn_ev[(ee, dout, n2)] = P.op(V, lambda e, dout=dout, n2=n2: e.tensor_add(
                                    X2F[:, n2 * 8192 + dout * 512: n2 * 8192 + (dout + 1) * 512],
                                    X2F[:, n2 * 8192 + dout * 512: n2 * 8192 + (dout + 1) * 512],
                                    pD[n2][:, :]))
                P.barrier()
          P.join(SY)
          for sl in range(2):
            for t in range(NT):
                P.dma(SY, out[sl][t], X2F[:, sl * 8192 + t * 512: sl * 8192 + (t + 1) * 512], ctr=P.so)
          P.wait(SY, P.so)
        P.finalize(block)
    return nc


def _tail_noop(P, nc, out):
    _tail_copyzero(P, nc, out)


def _tail_copyzero(P, nc, out):
    with ExitStack() as s:
        Z = s.enter_context(nc.sbuf_tensor("Zf", [128, 512], F32))
        P.op("gpsimd", lambda e: e.memset(Z[:, :], 0.0), inc=False)
        P.join("sync")
        for sl in range(2):
            for t in range(NT):
                P.dma("sync", out[sl][t], Z[:, :], ctr=P.so)
        P.wait("sync", P.so)



def core_layout(j):
    own_a, own_b = j, 7 - j
    need = [c for c in range(8) if c <= 7 - j and c not in (own_a, own_b)]
    lt = [c for c in need if c < j]
    rest = [c for c in need if c >= j]
    a_extra = (lt + rest)[:3]
    b_extra = [c for c in need if c not in a_extra]
    assert len(b_extra) <= 3
    regions = [own_a] + a_extra + [None] * (3 - len(a_extra)) \
        + [own_b] + b_extra + [None] * (3 - len(b_extra))
    return regions


def host_inputs(raw, b, j):
    X = raw["L0_attn_X"]
    regions = core_layout(j)
    xT = np.zeros((8, 128, NT * 512), np.float32)
    for r, c in enumerate(regions):
        if c is not None:
            blk = X[b, c * 512:(c + 1) * 512, :].T
            xT[r] = blk.reshape(NT, 128, 512).transpose(1, 0, 2).reshape(128, NT * 512)

    def heads_w(W):
        return np.ascontiguousarray(
            W.reshape(NT, 128, NH, 128).transpose(2, 1, 0, 3).reshape(NH, 128, NT * 128))

    maska = np.zeros((128, 16), np.float32)
    maskb = np.zeros((128, 32), np.float32)
    for t in range(16):
        r = t // 4
        c = regions[r]
        if r == 0:
            v = 0.0
        else:
            v = 0.0 if (c is not None and c < j) else NEG
        maska[:, t] = v
    for t in range(32):
        r = t // 4
        c = regions[r]
        if r == 4:
            v = 0.0
        else:
            v = 0.0 if (c is not None and c < 7 - j) else NEG
        maskb[:, t] = v

    rd = (np.arange(512)[:, None] <= np.arange(512)[None, :]).astype(np.float32)
    rdg = np.ascontiguousarray(
        np.concatenate([rd[t * 128:(t + 1) * 128] for t in range(4)], axis=1)).astype(bf)

    wo_r = raw["L0_attn_Wo"].reshape(NH, 128, NT, 128).transpose(2, 1, 0, 3)
    d = {
        "xT": xT,
        "wq": heads_w(raw["L0_attn_Wq"]),
        "wk": heads_w(raw["L0_attn_Wk"]),
        "wv": heads_w(raw["L0_attn_Wv"]),
        "bqkv": np.ascontiguousarray(np.stack([
            raw["L0_attn_bq"].reshape(NH, 128).T,
            raw["L0_attn_bk"].reshape(NH, 128).T,
            raw["L0_attn_bv"].reshape(NH, 128).T])),
        "lngb": np.ascontiguousarray(np.stack([
            raw["L0_attn_ln_g"].reshape(NT, 128).T,
            raw["L0_attn_ln_b"].reshape(NT, 128).T])),
        "wo": np.ascontiguousarray(wo_r.reshape(NT, 128, NH * 128)),
        "boT": np.ascontiguousarray(raw["L0_attn_bo"].reshape(NT, 128).T),
        "rw": np.ascontiguousarray(raw["L0_ffn_router_w"].reshape(NT, 128, NE).transpose(1, 0, 2)),
        "wg": np.ascontiguousarray(raw["L0_ffn_gate_proj"].reshape(
            NE, NT, 128, 8, 128).transpose(0, 3, 2, 1, 4).reshape(NE, 8, 128, NT * 128)),
        "wu": np.ascontiguousarray(raw["L0_ffn_up_proj"].reshape(
            NE, NT, 128, 8, 128).transpose(0, 3, 2, 1, 4).reshape(NE, 8, 128, NT * 128)),
        "wd": np.ascontiguousarray(raw["L0_ffn_down_proj"].reshape(
            NE, 8, 128, NT, 128).transpose(0, 3, 2, 1, 4).reshape(NE, NT, 128, 8 * 128)),
        "eye": np.eye(128, dtype=bf),
        "eyef": np.eye(128, dtype=np.float32),
        "rdg": rdg,
        "mska": maska,
        "mskb": maskb,
    }
    return d


_NC_CACHE = []


def kernel(**inputs):
    if not _NC_CACHE:
        _NC_CACHE.append(build_program())
    nc = _NC_CACHE[0]
    in_maps = [host_inputs(inputs, c // 4, c % 4) for c in range(8)]
    res = run_bass_kernel_spmd(nc, in_maps, core_ids=list(range(8)))
    out = np.zeros((2, 4096, 2048), np.float32)
    for core in range(8):
        b, j = core // 4, core % 4
        o = res.results[core]["out"]
        out[b, j * 512:(j + 1) * 512, :] = o[0].reshape(2048, 512).T
        out[b, (7 - j) * 512:(8 - j) * 512, :] = o[1].reshape(2048, 512).T
    return out
